# version 11
# baseline (speedup 1.0000x reference)
import numpy as np
import ml_dtypes
from contextlib import ExitStack

import concourse.bass as bass
import concourse.mybir as mybir
from concourse.bass_utils import run_bass_kernel_spmd

F32 = mybir.dt.float32
BF16 = mybir.dt.bfloat16
AF = mybir.ActivationFunctionType
ALU = mybir.AluOpType
AX = mybir.AxisListType

D_MODEL = 2048
NCD = 16
D_INNER = 4096
NCDI = 32
I_FFN = 1408
NCI = 11
TWOI = 2816
NE = 8
EPS = 1e-5
GK = 8
NG = NCDI // GK
TC0 = 4

ENGS = ("tensor", "vector", "scalar", "sync", "gpsimd")


def tblks(n, w=512):
    return [(s, min(w, n - s)) for s in range(0, n, w)]


def dap(t, off, dims):
    return bass.AP(t, off, [list(d) for d in dims])


QUEUES = {"sw": 6, "sx": 3, "sst": 2}


class KB:

    def __init__(self, nc):
        self.nc = nc
        self.sems = {}
        self.cnt = {}
        self.q = None
        self.waited = {e: {} for e in ENGS}
        self.qlanes = {}
        self.qrr = {}

    def add_sem(self, stack, name):
        self.sems[name] = stack.enter_context(self.nc.semaphore(name))
        self.cnt[name] = 0

    def add_queue(self, stack, qname, nlanes):
        self.qlanes[qname] = []
        self.qrr[qname] = 0
        for i in range(nlanes):
            n = f"{qname}{i}"
            self.add_sem(stack, n)
            self.qlanes[qname].append(n)

    def begin(self):
        self.q = {e: [] for e in ENGS}

    def end(self, name):
        for qn, eng in (("sw", "gpsimd"), ("sx", "sync"), ("sst", "sync")):
            for lane in self.qlanes.get(qn, ()):
                self.wait(eng, (lane, self.cnt[lane]))
        q = self.q
        with self.nc.Block(name) as blk:
            for e in ENGS:
                lst = q[e]
                if lst:
                    def body(E, lst=tuple(lst)):
                        for f in lst:
                            f(E)
                    getattr(blk, e)(body)
        for e in ENGS:
            for s, v in self.cnt.items():
                self.waited[e][s] = max(self.waited[e].get(s, 0), v)
        self.q = None

    def wait(self, eng, ev):
        if not ev:
            return
        s, v = ev
        if self.waited[eng].get(s, 0) >= v:
            return
        self.waited[eng][s] = v
        self.q[eng].append(lambda E, h=self.sems[s], v=v: E.wait_ge(h, v))

    def op(self, eng, fn, waits=(), inc=None):
        for ev in waits:
            self.wait(eng, ev)
        if inc is None:
            self.q[eng].append(fn)
            return None
        self.cnt[inc] += 1
        v = self.cnt[inc]
        self.q[eng].append(lambda E, fn=fn, h=self.sems[inc]: fn(E).then_inc(h, 1))
        self.waited[eng][inc] = v
        return (inc, v)

    def dma(self, eng, qname, out, in_, waits=()):
        lanes = self.qlanes[qname]
        lane = lanes[self.qrr[qname] % len(lanes)]
        self.qrr[qname] += 1
        for ev in waits:
            self.wait(eng, ev)
        self.wait(eng, (lane, self.cnt[lane]))
        self.cnt[lane] += 16
        v = self.cnt[lane]
        self.q[eng].append(
            lambda E, out=out, in_=in_, h=self.sems[lane]: E.dma_start(out, in_).then_inc(h, 16))
        return (lane, v)

    def drain(self, eng):
        self.q[eng].append(lambda E: E.drain())


class Pool:
    def __init__(self, tensors):
        self.slots = [{"t": t, "ev": None} for t in tensors]
        self.i = 0

    def acquire(self):
        s = self.slots[self.i % len(self.slots)]
        self.i += 1
        return s


def mm_group(kb, psum_ap, pairs, waits_first=()):
    n = len(pairs)
    ev = None
    for idx, (lt, rs, ws) in enumerate(pairs):
        waits = list(ws) + (list(waits_first) if idx == 0 else [])
        ev = kb.op(
            "tensor",
            lambda E, o=psum_ap, l=lt, r=rs, st=(idx == 0), sp=(idx == n - 1):
                E.matmul(o, l, r, start=st, stop=sp),
            waits=waits,
            inc="pe" if idx == n - 1 else None)
    return ev


def transp(kb, out_ap, in_ap, ident_ap, waits=()):
    return kb.op(
        "tensor",
        lambda E, o=out_ap, i=in_ap, d=ident_ap: E.transpose(o, i, d),
        waits=waits, inc="pe")


def vcopy(kb, dst, src, waits=()):
    return kb.op("vector", lambda E, d=dst, s=src: E.tensor_copy(d, s),
                 waits=waits, inc="v")


def build_nc(ntok=2048, dbg_skip=()):
    assert ntok % 128 == 0
    NT = ntok // 128
    W = ntok + TC0

    nc = bass.Bass("TRN2", target_bir_lowering=False)

    xfm_d = nc.dram_tensor("xfm", [D_MODEL, ntok + TC0], F32, kind="ExternalInput")
    w_inT = nc.dram_tensor("w_inT", [D_MODEL, 2 * D_INNER], F32, kind="ExternalInput")
    w_outT = nc.dram_tensor("w_outT", [D_INNER, D_MODEL], F32, kind="ExternalInput")
    w1_d = nc.dram_tensor("w1", [NE, D_MODEL, TWOI], F32, kind="ExternalInput")
    w2_d = nc.dram_tensor("w2", [NE, I_FFN, D_MODEL], F32, kind="ExternalInput")
    sw1T = nc.dram_tensor("sw1T", [D_MODEL, I_FFN], F32, kind="ExternalInput")
    sw2T = nc.dram_tensor("sw2T", [D_MODEL, I_FFN], F32, kind="ExternalInput")
    sw3T = nc.dram_tensor("sw3T", [I_FFN, D_MODEL], F32, kind="ExternalInput")
    gwT_d = nc.dram_tensor("gwT", [D_MODEL, NE], F32, kind="ExternalInput")
    cwk_d = nc.dram_tensor("cwk", [128, 128], F32, kind="ExternalInput")
    cb_d = nc.dram_tensor("cbcol", [128, 32], F32, kind="ExternalInput")
    dd_d = nc.dram_tensor("dcol", [128, 32], F32, kind="ExternalInput")
    idf_d = nc.dram_tensor("identf", [128, 128], F32, kind="ExternalInput")
    idb_d = nc.dram_tensor("identb", [128, 128], BF16, kind="ExternalInput")
    y_d = nc.dram_tensor("y", [D_MODEL, ntok], F32, kind="ExternalOutput")
    rows_d = nc.dram_tensor("rows_scratch", [7, ntok + TC0], F32)
    rows8_d = nc.dram_tensor("rows8_scratch", [NE, ntok], BF16)

    kb = KB(nc)
    with ExitStack() as st_:
        for s in ("pe", "v", "sc"):
            kb.add_sem(st_, s)
        for qn, nl in QUEUES.items():
            kb.add_queue(st_, qn, nl)

        A = [st_.enter_context(nc.sbuf_tensor(f"A{c}", [128, W], BF16)) for c in range(NCD)]
        B = [st_.enter_context(nc.sbuf_tensor(f"B{c}", [128, ntok], BF16)) for c in range(NCD)]
        identf = st_.enter_context(nc.sbuf_tensor("identf_s", [128, 128], F32))
        identb = st_.enter_context(nc.sbuf_tensor("identb_s", [128, 128], BF16))
        ones_c = st_.enter_context(nc.sbuf_tensor("ones_c", [128, 1], BF16))
        cwk = st_.enter_context(nc.sbuf_tensor("cwk_s", [128, 128], F32))
        cbc = st_.enter_context(nc.sbuf_tensor("cbc", [128, 32], F32))
        ddc = st_.enter_context(nc.sbuf_tensor("ddc", [128, 32], F32))
        epsc = st_.enter_context(nc.sbuf_tensor("epsc", [128, 1], F32))
        wxi = [st_.enter_context(nc.sbuf_tensor(f"wxi{i}", [128, NCD * 128], BF16)) for i in range(2)]
        wga = [st_.enter_context(nc.sbuf_tensor(f"wga{i}", [128, NCD * 128], BF16)) for i in range(2)]


        pmT = [st_.enter_context(nc.psum_tensor(f"pm{i}", [128, 512], F32)) for i in range(8)]
        pm = Pool(pmT)

        with ExitStack() as p01:
          with ExitStack() as p0:
            sqf = [p01.enter_context(nc.sbuf_tensor(f"sqf{i}", [128, W], BF16)) for i in range(2)]
            ba0 = p01.enter_context(nc.sbuf_tensor("ba0", [128, W], BF16))
            bb_0 = p01.enter_context(nc.sbuf_tensor("bb_0", [128, W], BF16))
            rA0 = p0.enter_context(nc.sbuf_tensor("rA0", [1, W], F32))
            rB0 = p0.enter_context(nc.sbuf_tensor("rB0", [1, W], F32))
            pfS = p0.enter_context(nc.sbuf_tensor("pfS", [128, 32], F32))
            pfQ = p0.enter_context(nc.sbuf_tensor("pfQ", [128, 32], F32))
            pfT = p0.enter_context(nc.sbuf_tensor("pfT", [128, 32], F32))
            qrem = p0.enter_context(nc.sbuf_tensor("qrem", [128, 64], BF16))

            kb.begin()
            ev_id1 = kb.dma("sync", "sx", identf[:, :], idf_d[:, :])
            ev_id2 = kb.dma("sync", "sx", identb[:, :], idb_d[:, :])
            ev_cwk = kb.dma("sync", "sx", cwk[:, :], cwk_d[:, :])
            ev_cb = kb.dma("sync", "sx", cbc[:, :], cb_d[:, :])
            ev_dd = kb.dma("sync", "sx", ddc[:, :], dd_d[:, :])
            ev_eps = kb.op("vector", lambda E: E.memset(epsc[:, :], EPS), inc="v")
            ev_ones = kb.op("vector", lambda E: E.memset(ones_c[:, :], 1.0), inc="v")

            ld_ev = []
            for c in range(NCD):
                ld_ev.append(kb.dma("gpsimd", "sw", A[c][:, :],
                                    xfm_d[c * 128:(c + 1) * 128, :]))
            pf_xi = kb.dma("gpsimd", "sw",
                           wxi[0][:, :].rearrange("p (k j) -> p k j", j=128),
                           dap(w_inT, 0, [[2 * D_INNER, 128], [128 * 2 * D_INNER, NCD], [1, 128]]))
            pf_ga = kb.dma("gpsimd", "sw",
                           wga[0][:, :].rearrange("p (k j) -> p k j", j=128),
                           dap(w_inT, D_INNER, [[2 * D_INNER, 128], [128 * 2 * D_INNER, NCD], [1, 128]]))
            main_w = min(W - W % 512, W) if W > 512 else W
            mblks = tblks(main_w)
            slots_x = [pm.acquire() for _ in range(len(mblks))]
            slots_q = [pm.acquire() for _ in range(len(mblks))]
            sq_war = [None, None]
            sq_evs = [None] * NCD
            qr_evs = [None] * NCD
            for k in range(NCD):
                qb = sqf[k % 2]
                if k % 2 == 0:
                    ev_q = kb.op("scalar",
                                 lambda E, k=k, qb=qb: E.activation(qb[:, :], A[k][:, :],
                                                                    AF.Square),
                                 waits=[ld_ev[k], sq_war[k % 2]], inc="sc")
                else:
                    kb.drain("vector")
                    ev_q = kb.op("vector",
                                 lambda E, k=k, qb=qb: E.tensor_mul(qb[:, :], A[k][:, :],
                                                                    A[k][:, :]),
                                 waits=[ld_ev[k], sq_war[k % 2]], inc="v")
                sq_evs[k] = ev_q
                for bidx, (t0, tw) in enumerate(mblks):
                    kb.op("tensor",
                          lambda E, o=slots_x[bidx]["t"][0:1, 0:tw],
                                 r=A[k][:, t0:t0 + tw], st=(k == 0), sp=(k == NCD - 1):
                              E.matmul(o, ones_c[:, 0:1], r, start=st, stop=sp),
                          waits=[ld_ev[k], ev_ones] + ([slots_x[bidx]["ev"]] if k == 0 else []),
                          inc="pe" if k == NCD - 1 else None)
                    if k == NCD - 1:
                        ev_e = vcopy(kb, rA0[0:1, t0:t0 + tw], slots_x[bidx]["t"][0:1, 0:tw],
                                     waits=[("pe", kb.cnt["pe"])])
                        slots_x[bidx]["ev"] = ev_e
                    ev_pe = kb.op(
                        "tensor",
                        lambda E, o=slots_q[bidx]["t"][0:1, 0:tw],
                               r=qb[:, t0:t0 + tw], st=(k == 0), sp=(k == NCD - 1):
                            E.matmul(o, ones_c[:, 0:1], r, start=st, stop=sp),
                        waits=[ev_q] + ([slots_q[bidx]["ev"]] if k == 0 else []),
                        inc="pe")
                    if k == NCD - 1:
                        ev_e = vcopy(kb, rB0[0:1, t0:t0 + tw], slots_q[bidx]["t"][0:1, 0:tw],
                                     waits=[ev_pe])
                        slots_q[bidx]["ev"] = ev_e
                sq_war[k % 2] = ev_pe
                if main_w < W:
                    kb.drain("vector")
                    qr_evs[k] = kb.op("vector",
                                      lambda E, k=k, twr=W - main_w: E.tensor_mul(
                                          qrem[:, k * twr:(k + 1) * twr],
                                          A[k][:, main_w:W], A[k][:, main_w:W]),
                                      waits=[ld_ev[k]], inc="v")
            if main_w < W:
                t0r, twr = main_w, W - main_w
                s_x = pm.acquire()
                s_q = pm.acquire()
                for k in range(NCD):
                    kb.op("tensor",
                          lambda E, o=s_x["t"][0:1, 0:twr], r=A[k][:, t0r:t0r + twr],
                                 st=(k == 0), sp=(k == NCD - 1):
                              E.matmul(o, ones_c[:, 0:1], r, start=st, stop=sp),
                          waits=[s_x["ev"]] if k == 0 else [],
                          inc="pe" if k == NCD - 1 else None)
                    if k == NCD - 1:
                        ev_e = vcopy(kb, rA0[0:1, t0r:t0r + twr], s_x["t"][0:1, 0:twr],
                                     waits=[("pe", kb.cnt["pe"])])
                        s_x["ev"] = ev_e
                    ev_pe = kb.op(
                        "tensor",
                        lambda E, o=s_q["t"][0:1, 0:twr],
                               st=(k == 0), sp=(k == NCD - 1), k=k:
                            E.matmul(o, ones_c[:, 0:1], qrem[:, k * twr:(k + 1) * twr],
                                     start=st, stop=sp),
                        waits=[qr_evs[k]] + ([s_q["ev"]] if k == 0 else []),
                        inc="pe")
                    if k == NCD - 1:
                        ev_e = vcopy(kb, rB0[0:1, t0r:t0r + twr], s_q["t"][0:1, 0:twr],
                                     waits=[ev_pe])
                        s_q["ev"] = ev_e
            FF = next(f for f in range(max(1, W // 128), W + 1) if W % f == 0 and W // f <= 128)
            PP = W // FF
            ev_stS = kb.dma("gpsimd", "sw", dap(rows_d, 5 * W, [[1, W]]), rA0[0:1, 0:W],
                            waits=[("v", kb.cnt["v"])])
            ev_stQ = kb.dma("gpsimd", "sw", dap(rows_d, 6 * W, [[1, W]]), rB0[0:1, 0:W],
                            waits=[("v", kb.cnt["v"])])
            ev_ldS = kb.dma("gpsimd", "sw", pfS[0:PP, 0:FF],
                            dap(rows_d, 5 * W, [[FF, PP], [1, FF]]), waits=[ev_stS])
            ev_ldQ = kb.dma("gpsimd", "sw", pfQ[0:PP, 0:FF],
                            dap(rows_d, 6 * W, [[FF, PP], [1, FF]]), waits=[ev_stQ])
            kb.op("vector", lambda E: E.tensor_scalar(pfS[0:PP, 0:FF], pfS[0:PP, 0:FF],
                                                      1.0 / D_MODEL, None, ALU.mult),
                  waits=[ev_ldS])
            kb.op("vector", lambda E: E.tensor_scalar(pfQ[0:PP, 0:FF], pfQ[0:PP, 0:FF],
                                                      1.0 / D_MODEL, None, ALU.mult),
                  waits=[ev_ldQ])
            kb.drain("vector")
            ev_m1 = kb.op("vector", lambda E: E.tensor_mul(pfT[0:PP, 0:FF], pfS[0:PP, 0:FF],
                                                           pfS[0:PP, 0:FF]), inc="v")
            ev_stm = kb.dma("gpsimd", "sw", dap(rows_d, 0, [[FF, PP], [1, FF]]),
                            pfS[0:PP, 0:FF], waits=[ev_m1])
            kb.drain("vector")
            ev_var = kb.op("vector",
                           lambda E: E.tensor_sub(pfQ[0:PP, 0:FF], pfQ[0:PP, 0:FF], pfT[0:PP, 0:FF]),
                           inc="v")
            ev_s1 = kb.op("scalar", lambda E: E.activation(pfQ[0:PP, 0:FF], pfQ[0:PP, 0:FF], AF.Sqrt,
                                                           bias=epsc[0:PP, 0:1], scale=1.0),
                          waits=[ev_var, ev_eps], inc="sc")
            ev_sts = kb.dma("gpsimd", "sw", dap(rows_d, W, [[FF, PP], [1, FF]]),
                            pfQ[0:PP, 0:FF], waits=[ev_s1])
            ev_r1 = kb.op("vector", lambda E: E.reciprocal(pfT[0:PP, 0:FF], pfQ[0:PP, 0:FF]),
                          waits=[ev_s1], inc="v")
            ev_str = kb.dma("gpsimd", "sw", dap(rows_d, 4 * W, [[FF, PP], [1, FF]]),
                            pfT[0:PP, 0:FF], waits=[ev_r1])
            ev_ba = kb.dma("gpsimd", "sw", ba0[:, :], dap(rows_d, 0, [[0, 128], [1, W]]),
                           waits=[ev_stm])
            ev_bb = kb.dma("gpsimd", "sw", bb_0[:, :], dap(rows_d, 4 * W, [[0, 128], [1, W]]),
                           waits=[ev_str])
            kb.end("p0")

        with ExitStack() as p1:
            wo = [p1.enter_context(nc.sbuf_tensor(f"wo{i}", [128, GK * 128], BF16)) for i in range(2)]
            xi_sb = p1.enter_context(nc.sbuf_tensor("xi_sb", [128, W], BF16))
            cacc = p1.enter_context(nc.sbuf_tensor("cacc", [128, ntok], BF16))
            sg_m = p1.enter_context(nc.sbuf_tensor("sg_m", [128, ntok], BF16))
            sxc = p1.enter_context(nc.sbuf_tensor("sxc", [128, ntok], BF16))
            H2 = [p1.enter_context(nc.sbuf_tensor(f"H2_{i}", [128, ntok], BF16)) for i in range(GK)]

            kb.begin()
            kb.wait("vector", ev_cwk)
            kb.wait("scalar", ev_cb)
            kb.wait("vector", ev_dd)
            app_ev = [None] * NCD
            for c in range(NCD):
                kb.drain("vector")
                kb.op("vector",
                      lambda E, c=c, t=sqf[c % 2]: E.tensor_sub(t[:, :], A[c][:, :], ba0[:, :]))
                kb.drain("vector")
                app_ev[c] = kb.op("vector",
                                  lambda E, c=c, t=sqf[c % 2]: E.tensor_mul(A[c][:, :], t[:, :],
                                                                            bb_0[:, :]),
                                  inc="v")
            wxi_cons = [None, None]
            wga_cons = [None, None]
            wo_cons = [None, None]
            h2_ev = [None] * GK
            h2_gate = None
            wo_pf = {}
            ev_sxc = None
            ev_h2_prev = None
            for g in range(NG):
                for dl in range(GK):
                    if dl == GK - 1:
                        for co_p in (0, 1):
                            wo_pf[co_p] = kb.dma(
                                "gpsimd", "sw",
                                wo[co_p % 2][:, :].rearrange("p (k j) -> p k j", j=128),
                                dap(w_outT, (g * GK * 128) * D_MODEL + co_p * 128,
                                    [[D_MODEL, 128], [128 * D_MODEL, GK], [1, 128]]),
                                waits=[wo_cons[co_p % 2]])
                    dc = g * GK + dl
                    bi = dc % 2
                    if dc == 0:
                        ld_xi, ld_ga = pf_xi, pf_ga
                    else:
                        ld_xi = kb.dma(
                            "gpsimd", "sw",
                            wxi[bi][:, :].rearrange("p (k j) -> p k j", j=128),
                            dap(w_inT, dc * 128,
                                [[2 * D_INNER, 128], [128 * 2 * D_INNER, NCD], [1, 128]]),
                            waits=[wxi_cons[bi]])
                        ld_ga = kb.dma(
                            "gpsimd", "sw",
                            wga[bi][:, :].rearrange("p (k j) -> p k j", j=128),
                            dap(w_inT, D_INNER + dc * 128,
                                [[2 * D_INNER, 128], [128 * 2 * D_INNER, NCD], [1, 128]]),
                            waits=[wga_cons[bi]])
                    xi_evs = []
                    last_pe = None
                    for (t0, tw) in tblks(W):
                        slot = pm.acquire()
                        pairs = [(wxi[bi][:, k * 128:(k + 1) * 128], A[k][:, t0:t0 + tw],
                                  [ld_xi, app_ev[k]]) for k in range(NCD)]
                        ev_pe = mm_group(kb, slot["t"][:, 0:tw], pairs, waits_first=[slot["ev"]])
                        ev_e = vcopy(kb, xi_sb[:, t0:t0 + tw], slot["t"][:, 0:tw], waits=[ev_pe])
                        slot["ev"] = ev_e
                        xi_evs.append(ev_e)
                        last_pe = ev_pe
                    wxi_cons[bi] = last_pe
                    sg_evs = []
                    for (t0, tw) in tblks(ntok):
                        slot = pm.acquire()
                        pairs = [(wga[bi][:, k * 128:(k + 1) * 128],
                                  A[k][:, TC0 + t0:TC0 + t0 + tw], [ld_ga, app_ev[k]])
                                 for k in range(NCD)]
                        ev_pe = mm_group(kb, slot["t"][:, 0:tw], pairs, waits_first=[slot["ev"]])
                        ev_e = kb.op("scalar",
                                     lambda E, d=sg_m[:, t0:t0 + tw], s=slot["t"][:, 0:tw]:
                                         E.activation(d, s, AF.Silu),
                                     waits=[ev_pe, ev_h2_prev], inc="sc")
                        slot["ev"] = ev_e
                        sg_evs.append(ev_e)
                        last_pe = ev_pe
                    wga_cons[bi] = last_pe
                    kb.drain("vector")
                    kb.op("vector",
                          lambda E, dc=dc: E.tensor_single_scalar(
                              cacc[:, 0:ntok], xi_sb[:, 1:1 + ntok],
                              cwk[:, dc * 4:dc * 4 + 1], ALU.mult),
                          waits=[ev_sxc])
                    ev_conv = None
                    for kk in range(1, 4):
                        kb.drain("vector")
                        ev_conv = kb.op("vector",
                                        lambda E, dc=dc, kk=kk: E.scalar_tensor_tensor(
                                            cacc[:, 0:ntok], xi_sb[:, 1 + kk:1 + kk + ntok],
                                            cwk[:, dc * 4 + kk:dc * 4 + kk + 1], cacc[:, 0:ntok],
                                            ALU.mult, ALU.add),
                                        inc="v" if kk == 3 else None)
                    ev_sxc = kb.op("scalar",
                                   lambda E, dc=dc: E.activation(sxc[:, 0:ntok], cacc[:, 0:ntok],
                                                                 AF.Silu,
                                                                 bias=cbc[:, dc:dc + 1], scale=1.0),
                                   waits=[ev_conv, ev_h2_prev], inc="sc")
                    kb.drain("vector")
                    ev_h2 = kb.op("vector",
                                  lambda E, dl=dl, dc=dc: E.scalar_tensor_tensor(
                                      H2[dl][:, :], sxc[:, 0:ntok], ddc[:, dc:dc + 1],
                                      sg_m[:, 0:ntok], ALU.mult, ALU.mult),
                                  waits=[ev_sxc, h2_gate] + sg_evs, inc="v")
                    h2_ev[dl] = ev_h2
                    ev_h2_prev = ev_h2
                kb.drain("vector")
                last_pe = None
                for co in range(NCD):
                    bi = co % 2
                    if co in wo_pf:
                        ld_wo = wo_pf.pop(co)
                    else:
                        ld_wo = kb.dma(
                            "gpsimd", "sw",
                            wo[bi][:, :].rearrange("p (k j) -> p k j", j=128),
                            dap(w_outT, (g * GK * 128) * D_MODEL + co * 128,
                                [[D_MODEL, 128], [128 * D_MODEL, GK], [1, 128]]),
                            waits=[wo_cons[bi]])
                    for (t0, tw) in tblks(ntok):
                        slot = pm.acquire()
                        pairs = [(wo[bi][:, kk * 128:(kk + 1) * 128], H2[kk][:, t0:t0 + tw],
                                  [ld_wo, h2_ev[kk]]) for kk in range(GK)]
                        ev_pe = mm_group(kb, slot["t"][:, 0:tw], pairs, waits_first=[slot["ev"]])
                        if g == 0:
                            ev_e = vcopy(kb, B[co][:, t0:t0 + tw], slot["t"][:, 0:tw], waits=[ev_pe])
                        else:
                            ev_e = kb.op("vector",
                                         lambda E, d=B[co][:, t0:t0 + tw], s=slot["t"][:, 0:tw]:
                                             E.tensor_add(d, d, s),
                                         waits=[ev_pe], inc="v")
                        slot["ev"] = ev_e
                        last_pe = ev_pe
                    wo_cons[bi] = last_pe
                h2_gate = last_pe
            kb.end("p1")

        with ExitStack() as p2:
            bb0 = p2.enter_context(nc.sbuf_tensor("bb0", [128, ntok], BF16))
            bb1 = p2.enter_context(nc.sbuf_tensor("bb1", [128, ntok], BF16))
            trt = p2.enter_context(nc.sbuf_tensor("trt", [128, ntok], BF16))
            sqb = [p2.enter_context(nc.sbuf_tensor(f"sqb{i}", [128, ntok], BF16)) for i in range(2)]
            rowA = p2.enter_context(nc.sbuf_tensor("rowA", [1, ntok], F32))
            rowB = p2.enter_context(nc.sbuf_tensor("rowB", [1, ntok], F32))
            pf2S = p2.enter_context(nc.sbuf_tensor("pf2S", [128, 16], F32))
            pf2Q = p2.enter_context(nc.sbuf_tensor("pf2Q", [128, 16], F32))
            pf2T = p2.enter_context(nc.sbuf_tensor("pf2T", [128, 16], F32))
            gwsb = p2.enter_context(nc.sbuf_tensor("gwsb", [128, NCD * NE], BF16))
            cwf8 = p2.enter_context(nc.sbuf_tensor("cwf8s", [NE, ntok], BF16))
            lg = p2.enter_context(nc.sbuf_tensor("lg", [NE, ntok], F32))
            ltw = p2.enter_context(nc.sbuf_tensor("ltw", [128, NT * NE], F32))
            msw = p2.enter_context(nc.sbuf_tensor("msw", [128, NT * NE], F32))
            ctw = p2.enter_context(nc.sbuf_tensor("ctw", [128, NT * NE], F32))
            gs = p2.enter_context(nc.sbuf_tensor("gs", [128, 3 * NT], F32))

            kb.begin()
            ev_gw = kb.dma("gpsimd", "sw",
                           gwsb[:, :].rearrange("p (k e) -> p k e", e=NE),
                           dap(gwT_d, 0, [[NE, 128], [128 * NE, NCD], [1, NE]]))
            pf_s1 = kb.dma("gpsimd", "sw",
                           wxi[0][:, :].rearrange("p (k j) -> p k j", j=128),
                           dap(sw1T, 0, [[I_FFN, 128], [128 * I_FFN, NCD], [1, 128]]))
            pf_s2 = kb.dma("gpsimd", "sw",
                           wga[0][:, :].rearrange("p (k j) -> p k j", j=128),
                           dap(sw2T, 0, [[I_FFN, 128], [128 * I_FFN, NCD], [1, 128]]))

            def bcast(row_idx, dst, waits=()):
                return [kb.dma("gpsimd", "sw", dst[:, :],
                               dap(rows_d, row_idx * W + TC0, [[0, 128], [1, ntok]]),
                               waits=list(waits))]

            ev_s1b = bcast(1, bb0, waits=[ev_sts])
            ev_m1b = bcast(0, bb1, waits=[ev_stm])
            recon_ev = [None] * NCD
            for c in range(NCD):
                kb.drain("vector")
                kb.op("vector",
                      lambda E, c=c: E.tensor_mul(trt[:, :], A[c][:, TC0:TC0 + ntok], bb0[:, :]),
                      waits=ev_s1b)
                kb.drain("vector")
                kb.op("vector", lambda E: E.tensor_add(trt[:, :], trt[:, :], bb1[:, :]),
                      waits=ev_m1b)
                kb.drain("vector")
                recon_ev[c] = kb.op("vector",
                                    lambda E, c=c: E.tensor_add(B[c][:, :], B[c][:, :], trt[:, :]),
                                    inc="v")
            kb.drain("vector")
            blks = tblks(ntok)
            nblk = len(blks)
            slots_x = [pm.acquire() for _ in range(nblk)]
            slots_q = [pm.acquire() for _ in range(nblk)]
            war_sq = [None, None]
            for k in range(NCD):
                qb = sqb[k % 2]
                if k % 2 == 0:
                    ev_q = kb.op("scalar",
                                 lambda E, k=k, qb=qb: E.activation(qb[:, :], B[k][:, :],
                                                                    AF.Square),
                                 waits=[recon_ev[k], war_sq[k % 2]], inc="sc")
                else:
                    kb.drain("vector")
                    ev_q = kb.op("vector",
                                 lambda E, k=k, qb=qb: E.tensor_mul(qb[:, :], B[k][:, :],
                                                                    B[k][:, :]),
                                 waits=[recon_ev[k], war_sq[k % 2]], inc="v")
                for bidx, (t0, tw) in enumerate(blks):
                    kb.op("tensor",
                          lambda E, o=slots_x[bidx]["t"][0:1, 0:tw],
                                 r=B[k][:, t0:t0 + tw], st=(k == 0), sp=(k == NCD - 1):
                              E.matmul(o, ones_c[:, 0:1], r, start=st, stop=sp),
                          waits=[recon_ev[k]] + ([slots_x[bidx]["ev"]] if k == 0 else []),
                          inc="pe" if k == NCD - 1 else None)
                    if k == NCD - 1:
                        ev_e = vcopy(kb, rowA[0:1, t0:t0 + tw], slots_x[bidx]["t"][0:1, 0:tw],
                                     waits=[("pe", kb.cnt["pe"])])
                        slots_x[bidx]["ev"] = ev_e
                    ev_pe = kb.op(
                        "tensor",
                        lambda E, o=slots_q[bidx]["t"][0:1, 0:tw],
                               r=qb[:, t0:t0 + tw], st=(k == 0), sp=(k == NCD - 1):
                            E.matmul(o, ones_c[:, 0:1], r, start=st, stop=sp),
                        waits=[ev_q] + ([slots_q[bidx]["ev"]] if k == 0 else []),
                        inc="pe")
                    if k == NCD - 1:
                        ev_e = vcopy(kb, rowB[0:1, t0:t0 + tw], slots_q[bidx]["t"][0:1, 0:tw],
                                     waits=[ev_pe])
                        slots_q[bidx]["ev"] = ev_e
                war_sq[k % 2] = ev_pe
            ev_stS2 = kb.dma("gpsimd", "sw", dap(rows_d, 5 * W, [[1, ntok]]), rowA[0:1, :],
                             waits=[("v", kb.cnt["v"])])
            ev_stQ2 = kb.dma("gpsimd", "sw", dap(rows_d, 6 * W, [[1, ntok]]), rowB[0:1, :],
                             waits=[("v", kb.cnt["v"])])
            ev_ldS2 = kb.dma("gpsimd", "sw", pf2S[0:128, 0:NT],
                             dap(rows_d, 5 * W, [[NT, 128], [1, NT]]), waits=[ev_stS2])
            ev_ldQ2 = kb.dma("gpsimd", "sw", pf2Q[0:128, 0:NT],
                             dap(rows_d, 6 * W, [[NT, 128], [1, NT]]), waits=[ev_stQ2])
            kb.op("vector", lambda E: E.tensor_scalar(pf2S[:, 0:NT], pf2S[:, 0:NT],
                                                      1.0 / D_MODEL, None, ALU.mult),
                  waits=[ev_ldS2])
            kb.op("vector", lambda E: E.tensor_scalar(pf2Q[:, 0:NT], pf2Q[:, 0:NT],
                                                      1.0 / D_MODEL, None, ALU.mult),
                  waits=[ev_ldQ2])
            kb.drain("vector")
            ev_m2 = kb.op("vector", lambda E: E.tensor_mul(pf2T[:, 0:NT], pf2S[:, 0:NT],
                                                           pf2S[:, 0:NT]), inc="v")
            ev_st_m2 = kb.dma("gpsimd", "sw", dap(rows_d, 2 * W + TC0, [[NT, 128], [1, NT]]),
                              pf2S[0:128, 0:NT], waits=[ev_m2])
            kb.drain("vector")
            ev_var = kb.op("vector",
                           lambda E: E.tensor_sub(pf2Q[:, 0:NT], pf2Q[:, 0:NT], pf2T[:, 0:NT]),
                           inc="v")
            ev_sr = kb.op("scalar", lambda E: E.activation(pf2Q[:, 0:NT], pf2Q[:, 0:NT], AF.Sqrt,
                                                           bias=epsc[0:128, 0:1], scale=1.0),
                          waits=[ev_var], inc="sc")
            ev_r2 = kb.op("vector", lambda E: E.reciprocal(pf2T[:, 0:NT], pf2Q[:, 0:NT]),
                          waits=[ev_sr], inc="v")
            ev_st_r2 = kb.dma("gpsimd", "sw", dap(rows_d, 3 * W + TC0, [[NT, 128], [1, NT]]),
                              pf2T[0:128, 0:NT], waits=[ev_r2])
            ev_m2b = bcast(2, bb1, waits=[ev_st_m2])
            ev_r2b = bcast(3, bb0, waits=[ev_st_r2])
            xn2_ev = [None] * NCD
            for c in range(NCD):
                kb.drain("vector")
                kb.op("vector", lambda E, c=c: E.tensor_sub(trt[:, :], B[c][:, :], bb1[:, :]),
                      waits=ev_m2b)
                kb.drain("vector")
                xn2_ev[c] = kb.op("vector",
                                  lambda E, c=c: E.tensor_mul(A[c][:, TC0:TC0 + ntok],
                                                              trt[:, :], bb0[:, :]),
                                  waits=ev_r2b, inc="v")
            lg_evs = []
            for (t0, tw) in tblks(ntok):
                slot = pm.acquire()
                pairs = [(gwsb[:, k * NE:(k + 1) * NE], A[k][:, TC0 + t0:TC0 + t0 + tw],
                          [ev_gw, xn2_ev[k]]) for k in range(NCD)]
                ev_pe = mm_group(kb, slot["t"][0:NE, 0:tw], pairs, waits_first=[slot["ev"]])
                ev_e = vcopy(kb, lg[0:NE, t0:t0 + tw], slot["t"][0:NE, 0:tw], waits=[ev_pe])
                slot["ev"] = ev_e
                lg_evs.append(ev_e)
            ev_ltw = None
            for i in range(NT):
                s_t = pm.acquire()
                ev_t = transp(kb, s_t["t"][:, 0:NE],
                              lg[0:NE, i * 128:(i + 1) * 128], identf[0:NE, 0:NE],
                              waits=[s_t["ev"], lg_evs[(i * 128) // 512]])
                ev_ltw = vcopy(kb, ltw[:, i * NE:(i + 1) * NE], s_t["t"][:, 0:NE], waits=[ev_t])
                s_t["ev"] = ev_ltw

            def r3(t):
                return t[:, 0:NT * NE].rearrange("p (i e) -> p i e", e=NE)

            def b3(gslice):
                return gslice.unsqueeze(2).broadcast_to([128, NT, NE])

            def vop(fn, waits=(), inc=None, dr=True):
                if dr:
                    kb.drain("vector")
                return kb.op("vector", fn, waits=waits, inc=inc)

            g0, g1, g2 = gs[:, 0:NT], gs[:, NT:2 * NT], gs[:, 2 * NT:3 * NT]
            vop(lambda E: E.reduce_max(g0, r3(ltw), axis=AX.X), waits=[ev_ltw])
            ev_sh = vop(lambda E: E.tensor_sub(r3(ltw), r3(ltw), b3(g0)), inc="v")
            ev_ex = kb.op("scalar", lambda E: E.activation(ltw[:, 0:NT * NE], ltw[:, 0:NT * NE],
                                                           AF.Exp),
                          waits=[ev_sh], inc="sc")
            vop(lambda E: E.reduce_sum(g1, r3(ltw), axis=AX.X), waits=[ev_ex], dr=False)
            vop(lambda E: E.reciprocal(g2, g1))
            ev_p = vop(lambda E: E.tensor_mul(r3(ltw), r3(ltw), b3(g2)), inc="v")
            vop(lambda E: E.reduce_max(g0, r3(ltw), axis=AX.X))
            vop(lambda E: E.tensor_tensor(r3(msw), r3(ltw), b3(g0), ALU.is_equal))
            vop(lambda E: E.scalar_tensor_tensor(ctw[:, 0:NT * NE], msw[:, 0:NT * NE], -1e30,
                                                 ltw[:, 0:NT * NE], ALU.mult, ALU.add))
            vop(lambda E: E.reduce_max(g1, r3(ctw), axis=AX.X))
            vop(lambda E: E.tensor_tensor(r3(ctw), r3(ctw), b3(g1), ALU.is_equal))
            vop(lambda E: E.tensor_add(g2, g0, g1))
            vop(lambda E: E.tensor_scalar(g2, g2, 1e-20, None, ALU.add))
            vop(lambda E: E.reciprocal(g2, g2))
            vop(lambda E: E.tensor_mul(g0, g0, g2))
            kb.op("vector", lambda E: E.tensor_mul(g1, g1, g2))
            vop(lambda E: E.tensor_mul(r3(msw), r3(msw), b3(g0)))
            kb.op("vector", lambda E: E.tensor_mul(r3(ctw), r3(ctw), b3(g1)))
            ev_cw = vop(lambda E: E.tensor_add(msw[:, 0:NT * NE], msw[:, 0:NT * NE],
                                               ctw[:, 0:NT * NE]), inc="v")
            for i in range(NT):
                s_b = pm.acquire()
                ev_t2 = transp(kb, s_b["t"][0:NE, 0:128], msw[:, i * NE:(i + 1) * NE],
                               identf[0:128, 0:128], waits=[s_b["ev"], ev_cw])
                ev_e2 = vcopy(kb, cwf8[0:NE, i * 128:(i + 1) * 128], s_b["t"][0:NE, 0:128],
                              waits=[ev_t2])
                s_b["ev"] = ev_e2
            ev_cwf = kb.dma("gpsimd", "sw", rows8_d[0:NE, 0:ntok], cwf8[0:NE, 0:ntok],
                            waits=[("v", kb.cnt["v"])])
            kb.end("p2")

        with ExitStack() as p3:
            P = [p3.enter_context(nc.sbuf_tensor(f"P{i}", [128, ntok], BF16)) for i in range(NCI)]
            sgE = [p3.enter_context(nc.sbuf_tensor(f"sgE{i}", [128, ntok], BF16)) for i in range(2)]
            cwb = p3.enter_context(nc.sbuf_tensor("cwb", [128, ntok], BF16))
            w2b = [p3.enter_context(nc.sbuf_tensor(f"w2b_{i}", [128, NCI * 128], BF16)) for i in range(2)]

            kb.begin()
            wg_cons = [None, None]
            wu_cons = [None, None]
            w2_cons = [None, None]
            sg_war = [None, None]
            st_war = [None, None]
            ev_he_last = None
            ev_sg = None
            ev_pmul_prev = None
            for ic in range(NCI if 'shared' not in dbg_skip else 0):
                bi = ic % 2
                if ic == 0:
                    ld1, ld2 = pf_s1, pf_s2
                else:
                    ld1 = kb.dma("gpsimd", "sw",
                                 wxi[bi][:, :].rearrange("p (k j) -> p k j", j=128),
                                 dap(sw1T, ic * 128, [[I_FFN, 128], [128 * I_FFN, NCD], [1, 128]]),
                                 waits=[wg_cons[bi]])
                    ld2 = kb.dma("gpsimd", "sw",
                                 wga[bi][:, :].rearrange("p (k j) -> p k j", j=128),
                                 dap(sw2T, ic * 128, [[I_FFN, 128], [128 * I_FFN, NCD], [1, 128]]),
                                 waits=[wu_cons[bi]])
                last_pe = None
                for (t0, tw) in tblks(ntok):
                    slot = pm.acquire()
                    pairs = [(wxi[bi][:, k * 128:(k + 1) * 128],
                              A[k][:, TC0 + t0:TC0 + t0 + tw], [ld1]) for k in range(NCD)]
                    ev_pe = mm_group(kb, slot["t"][:, 0:tw], pairs, waits_first=[slot["ev"]])
                    sgb = sgE[ic % 2]
                    ev_e = kb.op("scalar",
                                 lambda E, d=sgb[:, t0:t0 + tw], s=slot["t"][:, 0:tw]:
                                     E.activation(d, s, AF.Silu),
                                 waits=[ev_pe, sg_war[ic % 2]], inc="sc")
                    slot["ev"] = ev_e
                    ev_sg = ev_e
                    last_pe = ev_pe
                wg_cons[bi] = last_pe
                for (t0, tw) in tblks(ntok):
                    slot = pm.acquire()
                    pairs = [(wga[bi][:, k * 128:(k + 1) * 128],
                              A[k][:, TC0 + t0:TC0 + t0 + tw], [ld2]) for k in range(NCD)]
                    ev_pe = mm_group(kb, slot["t"][:, 0:tw], pairs, waits_first=[slot["ev"]])
                    ev_e = vcopy(kb, P[ic][:, t0:t0 + tw], slot["t"][:, 0:tw], waits=[ev_pe])
                    slot["ev"] = ev_e
                    last_pe = ev_pe
                wu_cons[bi] = last_pe
                kb.drain("vector")
                ev_pmul_prev = kb.op("vector",
                                     lambda E, ic=ic, sgb=sgE[ic % 2]: E.tensor_mul(
                                         P[ic][:, :], P[ic][:, :], sgb[:, :]),
                                     waits=[ev_sg], inc="v")
                sg_war[ic % 2] = ev_pmul_prev
            p_gate = None
            kb.drain("vector")
            for co in range(NCD):
                bi = co % 2
                ld3 = kb.dma("gpsimd", "sw",
                             w2b[bi][:, :].rearrange("p (k j) -> p k j", j=128),
                             dap(sw3T, co * 128, [[D_MODEL, 128], [128 * D_MODEL, NCI], [1, 128]]),
                             waits=[w2_cons[bi]])
                for (t0, tw) in tblks(ntok):
                    slot = pm.acquire()
                    pairs = [(w2b[bi][:, k * 128:(k + 1) * 128], P[k][:, t0:t0 + tw],
                              [ld3, ev_pmul_prev]) for k in range(NCI)]
                    ev_pe = mm_group(kb, slot["t"][:, 0:tw], pairs, waits_first=[slot["ev"]])
                    ev_e = kb.op("vector",
                                 lambda E, d=B[co][:, t0:t0 + tw], s=slot["t"][:, 0:tw]:
                                     E.tensor_add(d, d, s),
                                 waits=[ev_pe], inc="v")
                    slot["ev"] = ev_e
                    p_gate = ev_pe
                w2_cons[bi] = p_gate
            ev_sg = None
            ne_run = NE
            for d in dbg_skip:
                if d.startswith('ne='):
                    ne_run = int(d[3:])
            if 'experts' in dbg_skip:
                ne_run = 0
            for e in range(ne_run):
                kb.drain("vector")
                cwb_evs = [kb.dma("gpsimd", "sw", cwb[:, :],
                                  dap(rows8_d, e * ntok, [[0, 128], [1, ntok]]),
                                  waits=[ev_cwf, ev_pmul_prev])]
                p_evs = [None] * NCI
                for J in range(NCI):
                    bi = J % 2
                    ldg = kb.dma("gpsimd", "sw",
                                 wxi[bi][:, :].rearrange("p (k j) -> p k j", j=128),
                                 dap(w1_d, e * D_MODEL * TWOI + J * 128,
                                     [[TWOI, 128], [128 * TWOI, NCD], [1, 128]]),
                                 waits=[wg_cons[bi]])
                    ldu = kb.dma("gpsimd", "sw",
                                 wga[bi][:, :].rearrange("p (k j) -> p k j", j=128),
                                 dap(w1_d, e * D_MODEL * TWOI + I_FFN + J * 128,
                                     [[TWOI, 128], [128 * TWOI, NCD], [1, 128]]),
                                 waits=[wu_cons[bi]])
                    last_pe = None
                    for (t0, tw) in tblks(ntok):
                        slot = pm.acquire()
                        pairs = [(wxi[bi][:, k * 128:(k + 1) * 128],
                                  A[k][:, TC0 + t0:TC0 + t0 + tw], [ldg]) for k in range(NCD)]
                        ev_pe = mm_group(kb, slot["t"][:, 0:tw], pairs, waits_first=[slot["ev"]])
                        sgb = sgE[J % 2]
                        ev_e = kb.op("scalar",
                                     lambda E, d=sgb[:, t0:t0 + tw], s=slot["t"][:, 0:tw]:
                                         E.activation(d, s, AF.Silu),
                                     waits=[ev_pe, sg_war[J % 2]], inc="sc")
                        slot["ev"] = ev_e
                        ev_sg = ev_e
                        last_pe = ev_pe
                    wg_cons[bi] = last_pe
                    for (t0, tw) in tblks(ntok):
                        slot = pm.acquire()
                        pairs = [(wga[bi][:, k * 128:(k + 1) * 128],
                                  A[k][:, TC0 + t0:TC0 + t0 + tw], [ldu]) for k in range(NCD)]
                        ev_pe = mm_group(kb, slot["t"][:, 0:tw], pairs, waits_first=[slot["ev"]])
                        ev_e = vcopy(kb, P[J][:, t0:t0 + tw], slot["t"][:, 0:tw],
                                     waits=[ev_pe, p_gate])
                        slot["ev"] = ev_e
                        last_pe = ev_pe
                    wu_cons[bi] = last_pe
                    ev_he_last = last_pe
                    kb.drain("vector")
                    kb.op("vector", lambda E, J=J, sgb=sgE[J % 2]: E.tensor_mul(
                              P[J][:, :], P[J][:, :], sgb[:, :]),
                          waits=[ev_sg], inc="v")
                    sg_war[J % 2] = (kb.cnt["v"] and ("v", kb.cnt["v"]))
                    kb.drain("vector")
                    ev_pm = kb.op("vector",
                                  lambda E, J=J: E.tensor_mul(P[J][:, :], P[J][:, :], cwb[:, :]),
                                  waits=cwb_evs, inc="v")
                    p_evs[J] = ev_pm
                    ev_pmul_prev = ev_pm
                kb.drain("vector")
                for cb_i in range(NCD):
                    bi = cb_i % 2
                    ld2 = kb.dma("gpsimd", "sw",
                                 w2b[bi][:, :].rearrange("p (k j) -> p k j", j=128),
                                 dap(w2_d, e * I_FFN * D_MODEL + cb_i * 128,
                                     [[D_MODEL, 128], [128 * D_MODEL, NCI], [1, 128]]),
                                 waits=[w2_cons[bi]])
                    for (t0, tw) in tblks(ntok):
                        slot = pm.acquire()
                        pairs = [(w2b[bi][:, k * 128:(k + 1) * 128], P[k][:, t0:t0 + tw],
                                  [ld2, p_evs[k]]) for k in range(NCI)]
                        ev_pe = mm_group(kb, slot["t"][:, 0:tw], pairs, waits_first=[slot["ev"]])
                        ev_e = kb.op("vector",
                                     lambda E, d=B[cb_i][:, t0:t0 + tw], s=slot["t"][:, 0:tw]:
                                         E.tensor_add(d, d, s),
                                     waits=[ev_pe], inc="v")
                        slot["ev"] = ev_e
                        p_gate = ev_pe
                    w2_cons[bi] = p_gate
                    if e == ne_run - 1:
                        view = A[cb_i][:, :].bitcast(F32)
                        fw = view.shape[1]
                        off = 0
                        while off < ntok:
                            w = min(fw, ntok - off)
                            kb.drain("vector")
                            ev_cast = kb.op(
                                "vector",
                                lambda E, d=view[:, 0:w], s=B[cb_i][:, off:off + w]:
                                    E.tensor_copy(d, s),
                                waits=[ev_e, ev_he_last, st_war[cb_i % 2]], inc="v")
                            st_war[cb_i % 2] = kb.dma(
                                "sync", "sst", y_d[cb_i * 128:(cb_i + 1) * 128, off:off + w],
                                view[:, 0:w], waits=[ev_cast])
                            off += w
            kb.end("p3")

        if ne_run != NE:
            kb.begin()
            for c in range(NCD):
                kb.dma("gpsimd", "sw", y_d[c * 128:(c + 1) * 128, :], B[c][:, :])
            kb.end("p4")

    return nc


def _stage(inputs):
    x = np.ascontiguousarray(np.asarray(inputs["L0_attn_x"], dtype=np.float32))
    w_in = np.asarray(inputs["L0_attn_w_in"], dtype=np.float32)
    w_out = np.asarray(inputs["L0_attn_w_out"], dtype=np.float32)
    conv_w = np.asarray(inputs["L0_attn_conv_w"], dtype=np.float32)
    conv_b = np.asarray(inputs["L0_attn_conv_b"], dtype=np.float32)
    dvec = np.asarray(inputs["L0_attn_D"], dtype=np.float32)
    gate_w = np.asarray(inputs["L0_ffn_gate_w"], dtype=np.float32)
    w1 = np.ascontiguousarray(np.asarray(inputs["L0_ffn_w1"], dtype=np.float32))
    w2 = np.ascontiguousarray(np.asarray(inputs["L0_ffn_w2"], dtype=np.float32))
    sw1 = np.asarray(inputs["L0_ffn_sw1"], dtype=np.float32)
    sw2 = np.asarray(inputs["L0_ffn_sw2"], dtype=np.float32)
    sw3 = np.asarray(inputs["L0_ffn_sw3"], dtype=np.float32)

    shared = {
        "w_inT": np.ascontiguousarray(w_in.T),
        "w_outT": np.ascontiguousarray(w_out.T),
        "w1": w1,
        "w2": w2,
        "sw1T": np.ascontiguousarray(sw1.T),
        "sw2T": np.ascontiguousarray(sw2.T),
        "sw3T": np.ascontiguousarray(sw3.T),
        "gwT": np.ascontiguousarray(gate_w.T),
        "cwk": np.ascontiguousarray(
            conv_w[:, 0, :].reshape(32, 128, 4).transpose(1, 0, 2).reshape(128, 128)),
        "cbcol": np.ascontiguousarray(conv_b.reshape(32, 128).T),
        "dcol": np.ascontiguousarray(dvec.reshape(32, 128).T),
        "identf": np.eye(128, dtype=np.float32),
        "identb": np.eye(128, dtype=ml_dtypes.bfloat16),
    }
    in_maps = []
    bsz, tfull, _ = x.shape
    half = tfull // 2
    for core in range(8):
        b, h = core // 2, core % 2
        t0 = h * half
        arr = np.zeros((half + 4, D_MODEL), np.float32)
        arr[4:] = x[b, t0:t0 + half]
        if t0 > 0:
            arr[1:4] = x[b, t0 - 3:t0]
        in_maps.append(dict(shared, xfm=np.ascontiguousarray(arr.T)))
    return in_maps


def kernel(**inputs):
    in_maps = _stage(inputs)
    ntok = in_maps[0]["xfm"].shape[1] - TC0
    nc = build_nc(ntok)
    res = run_bass_kernel_spmd(nc, in_maps, core_ids=list(range(8)))
    x = np.asarray(inputs["L0_attn_x"])
    bsz, tfull, dm = x.shape
    half = tfull // 2
    out = np.empty((bsz, tfull, dm), np.float32)
    for core, r in enumerate(res.results):
        b, h = core // 2, core % 2
        out[b, h * half:(h + 1) * half, :] = r["y"].T
    return out


# version 15
# speedup vs baseline: 1.0103x; 1.0103x over previous
import numpy as np
import ml_dtypes
from contextlib import ExitStack

import concourse.bass as bass
import concourse.mybir as mybir
from concourse.bass_utils import run_bass_kernel_spmd

F32 = mybir.dt.float32
BF16 = mybir.dt.bfloat16
AF = mybir.ActivationFunctionType
ALU = mybir.AluOpType
AX = mybir.AxisListType

D_MODEL = 2048
NCD = 16
D_INNER = 4096
NCDI = 32
I_FFN = 1408
NCI = 11
TWOI = 2816
NE = 8
EPS = 1e-5
GK = 8
NG = NCDI // GK
TC0 = 4

ENGS = ("tensor", "vector", "scalar", "sync", "gpsimd")


def tblks(n, w=512):
    return [(s, min(w, n - s)) for s in range(0, n, w)]


def dap(t, off, dims):
    return bass.AP(t, off, [list(d) for d in dims])


QUEUES = {"sw": 6, "sx": 3, "sst": 2}


class KB:

    def __init__(self, nc):
        self.nc = nc
        self.sems = {}
        self.cnt = {}
        self.q = None
        self.waited = {e: {} for e in ENGS}
        self.qlanes = {}
        self.qrr = {}

    def add_sem(self, stack, name):
        self.sems[name] = stack.enter_context(self.nc.semaphore(name))
        self.cnt[name] = 0

    def add_queue(self, stack, qname, nlanes):
        self.qlanes[qname] = []
        self.qrr[qname] = 0
        for i in range(nlanes):
            n = f"{qname}{i}"
            self.add_sem(stack, n)
            self.qlanes[qname].append(n)

    def begin(self):
        self.q = {e: [] for e in ENGS}

    def end(self, name):
        for qn, eng in (("sw", "gpsimd"), ("sx", "sync"), ("sst", "sync")):
            for lane in self.qlanes.get(qn, ()):
                self.wait(eng, (lane, self.cnt[lane]))
        q = self.q
        with self.nc.Block(name) as blk:
            for e in ENGS:
                lst = q[e]
                if lst:
                    def body(E, lst=tuple(lst)):
                        for f in lst:
                            f(E)
                    getattr(blk, e)(body)
        for e in ENGS:
            for s, v in self.cnt.items():
                self.waited[e][s] = max(self.waited[e].get(s, 0), v)
        self.q = None

    def wait(self, eng, ev):
        if not ev:
            return
        s, v = ev
        if self.waited[eng].get(s, 0) >= v:
            return
        self.waited[eng][s] = v
        self.q[eng].append(lambda E, h=self.sems[s], v=v: E.wait_ge(h, v))

    def op(self, eng, fn, waits=(), inc=None):
        for ev in waits:
            self.wait(eng, ev)
        if inc is None:
            self.q[eng].append(fn)
            return None
        self.cnt[inc] += 1
        v = self.cnt[inc]
        self.q[eng].append(lambda E, fn=fn, h=self.sems[inc]: fn(E).then_inc(h, 1))
        self.waited[eng][inc] = v
        return (inc, v)

    def dma(self, eng, qname, out, in_, waits=()):
        lanes = self.qlanes[qname]
        lane = lanes[self.qrr[qname] % len(lanes)]
        self.qrr[qname] += 1
        for ev in waits:
            self.wait(eng, ev)
        self.wait(eng, (lane, self.cnt[lane]))
        self.cnt[lane] += 16
        v = self.cnt[lane]
        self.q[eng].append(
            lambda E, out=out, in_=in_, h=self.sems[lane]: E.dma_start(out, in_).then_inc(h, 16))
        return (lane, v)

    def drain(self, eng):
        self.q[eng].append(lambda E: E.drain())


class Pool:
    def __init__(self, tensors):
        self.slots = [{"t": t, "ev": None} for t in tensors]
        self.i = 0

    def acquire(self):
        s = self.slots[self.i % len(self.slots)]
        self.i += 1
        return s


def mm_group(kb, psum_ap, pairs, waits_first=()):
    n = len(pairs)
    ev = None
    for idx, (lt, rs, ws) in enumerate(pairs):
        waits = list(ws) + (list(waits_first) if idx == 0 else [])
        ev = kb.op(
            "tensor",
            lambda E, o=psum_ap, l=lt, r=rs, st=(idx == 0), sp=(idx == n - 1):
                E.matmul(o, l, r, start=st, stop=sp),
            waits=waits,
            inc="pe" if idx == n - 1 else None)
    return ev


def transp(kb, out_ap, in_ap, ident_ap, waits=()):
    return kb.op(
        "tensor",
        lambda E, o=out_ap, i=in_ap, d=ident_ap: E.transpose(o, i, d),
        waits=waits, inc="pe")


def vcopy(kb, dst, src, waits=()):
    return kb.op("vector", lambda E, d=dst, s=src: E.tensor_copy(d, s),
                 waits=waits, inc="v")


def build_nc(ntok=2048, dbg_skip=()):
    assert ntok % 128 == 0
    NT = ntok // 128
    W = ntok + TC0

    nc = bass.Bass("TRN2", target_bir_lowering=False)

    xfm_d = nc.dram_tensor("xfm", [D_MODEL, ntok + TC0], F32, kind="ExternalInput")
    w_inT = nc.dram_tensor("w_inT", [D_MODEL, 2 * D_INNER], F32, kind="ExternalInput")
    w_outT = nc.dram_tensor("w_outT", [D_INNER, D_MODEL], F32, kind="ExternalInput")
    w1_d = nc.dram_tensor("w1", [NE, D_MODEL, TWOI], F32, kind="ExternalInput")
    w2_d = nc.dram_tensor("w2", [NE, I_FFN, D_MODEL], F32, kind="ExternalInput")
    sw1T = nc.dram_tensor("sw1T", [D_MODEL, I_FFN], F32, kind="ExternalInput")
    sw2T = nc.dram_tensor("sw2T", [D_MODEL, I_FFN], F32, kind="ExternalInput")
    sw3T = nc.dram_tensor("sw3T", [I_FFN, D_MODEL], F32, kind="ExternalInput")
    gwT_d = nc.dram_tensor("gwT", [D_MODEL, NE], F32, kind="ExternalInput")
    cwk_d = nc.dram_tensor("cwk", [128, 128], F32, kind="ExternalInput")
    cb_d = nc.dram_tensor("cbcol", [128, 32], F32, kind="ExternalInput")
    dd_d = nc.dram_tensor("dcol", [128, 32], F32, kind="ExternalInput")
    idf_d = nc.dram_tensor("identf", [128, 128], F32, kind="ExternalInput")
    idb_d = nc.dram_tensor("identb", [128, 128], BF16, kind="ExternalInput")
    y_d = nc.dram_tensor("y", [D_MODEL, ntok], F32, kind="ExternalOutput")
    rows_d = nc.dram_tensor("rows_scratch", [7, ntok + TC0], F32)
    rows8_d = nc.dram_tensor("rows8_scratch", [NE, ntok], BF16)

    kb = KB(nc)
    with ExitStack() as st_:
        for s in ("pe", "v", "sc"):
            kb.add_sem(st_, s)
        for qn, nl in QUEUES.items():
            kb.add_queue(st_, qn, nl)

        A = [st_.enter_context(nc.sbuf_tensor(f"A{c}", [128, W], BF16)) for c in range(NCD)]
        B = [st_.enter_context(nc.sbuf_tensor(f"B{c}", [128, ntok], BF16)) for c in range(NCD)]
        identf = st_.enter_context(nc.sbuf_tensor("identf_s", [128, 128], F32))
        identb = st_.enter_context(nc.sbuf_tensor("identb_s", [128, 128], BF16))
        ones_c = st_.enter_context(nc.sbuf_tensor("ones_c", [128, 1], BF16))
        ones_r = st_.enter_context(nc.sbuf_tensor("ones_r", [1, 128], BF16))
        cwk = st_.enter_context(nc.sbuf_tensor("cwk_s", [128, 128], F32))
        cbc = st_.enter_context(nc.sbuf_tensor("cbc", [128, 32], F32))
        ddc = st_.enter_context(nc.sbuf_tensor("ddc", [128, 32], F32))
        epsc = st_.enter_context(nc.sbuf_tensor("epsc", [128, 1], F32))
        wxi = [st_.enter_context(nc.sbuf_tensor(f"wxi{i}", [128, NCD * 128], BF16)) for i in range(2)]
        wga = [st_.enter_context(nc.sbuf_tensor(f"wga{i}", [128, NCD * 128], BF16)) for i in range(2)]


        pmT = [st_.enter_context(nc.psum_tensor(f"pm{i}", [128, 512], F32)) for i in range(8)]
        pm = Pool(pmT)

        p01 = ExitStack()
        ba0 = p01.enter_context(nc.sbuf_tensor("ba0", [128, W], BF16))
        bb_0 = p01.enter_context(nc.sbuf_tensor("bb_0", [128, W], BF16))
        s1bt = p01.enter_context(nc.sbuf_tensor("s1bt", [128, W], BF16))
        with ExitStack() as p0:
            sqf = [p0.enter_context(nc.sbuf_tensor(f"sqf{i}", [128, W], BF16)) for i in range(2)]
            rA0 = p0.enter_context(nc.sbuf_tensor("rA0", [1, W], F32))
            rB0 = p0.enter_context(nc.sbuf_tensor("rB0", [1, W], F32))
            pfS = p0.enter_context(nc.sbuf_tensor("pfS", [128, 32], F32))
            pfQ = p0.enter_context(nc.sbuf_tensor("pfQ", [128, 32], F32))
            pfT = p0.enter_context(nc.sbuf_tensor("pfT", [128, 32], F32))
            qrem = p0.enter_context(nc.sbuf_tensor("qrem", [128, 64], BF16))

            kb.begin()
            ev_id1 = kb.dma("sync", "sx", identf[:, :], idf_d[:, :])
            ev_id2 = kb.dma("sync", "sx", identb[:, :], idb_d[:, :])
            ev_cwk = kb.dma("sync", "sx", cwk[:, :], cwk_d[:, :])
            ev_cb = kb.dma("sync", "sx", cbc[:, :], cb_d[:, :])
            ev_dd = kb.dma("sync", "sx", ddc[:, :], dd_d[:, :])
            ev_eps = kb.op("vector", lambda E: E.memset(epsc[:, :], EPS), inc="v")
            ev_ones = kb.op("vector", lambda E: E.memset(ones_c[:, :], 1.0), inc="v")
            ev_ones_r = kb.op("vector", lambda E: E.memset(ones_r[:, :], 1.0), inc="v")

            ld_ev = []
            for c in range(NCD):
                ld_ev.append(kb.dma("gpsimd", "sw", A[c][:, :],
                                    xfm_d[c * 128:(c + 1) * 128, :]))
            pf_xi = kb.dma("gpsimd", "sw",
                           wxi[0][:, :].rearrange("p (k j) -> p k j", j=128),
                           dap(w_inT, 0, [[2 * D_INNER, 128], [128 * 2 * D_INNER, NCD], [1, 128]]))
            pf_ga = kb.dma("gpsimd", "sw",
                           wga[0][:, :].rearrange("p (k j) -> p k j", j=128),
                           dap(w_inT, D_INNER, [[2 * D_INNER, 128], [128 * 2 * D_INNER, NCD], [1, 128]]))
            main_w = min(W - W % 512, W) if W > 512 else W
            mblks = tblks(main_w)
            slots_x = [pm.acquire() for _ in range(len(mblks))]
            slots_q = [pm.acquire() for _ in range(len(mblks))]
            sq_war = [None, None]
            sq_evs = [None] * NCD
            qr_evs = [None] * NCD
            for k in range(NCD):
                qb = sqf[k % 2]
                if k % 2 == 0:
                    ev_q = kb.op("scalar",
                                 lambda E, k=k, qb=qb: E.activation(qb[:, :], A[k][:, :],
                                                                    AF.Square),
                                 waits=[ld_ev[k], sq_war[k % 2]], inc="sc")
                else:
                    kb.drain("vector")
                    ev_q = kb.op("vector",
                                 lambda E, k=k, qb=qb: E.tensor_mul(qb[:, :], A[k][:, :],
                                                                    A[k][:, :]),
                                 waits=[ld_ev[k], sq_war[k % 2]], inc="v")
                sq_evs[k] = ev_q
                for bidx, (t0, tw) in enumerate(mblks):
                    kb.op("tensor",
                          lambda E, o=slots_x[bidx]["t"][0:1, 0:tw],
                                 r=A[k][:, t0:t0 + tw], st=(k == 0), sp=(k == NCD - 1):
                              E.matmul(o, ones_c[:, 0:1], r, start=st, stop=sp),
                          waits=[ld_ev[k], ev_ones] + ([slots_x[bidx]["ev"]] if k == 0 else []),
                          inc="pe" if k == NCD - 1 else None)
                    if k == NCD - 1:
                        ev_e = vcopy(kb, rA0[0:1, t0:t0 + tw], slots_x[bidx]["t"][0:1, 0:tw],
                                     waits=[("pe", kb.cnt["pe"])])
                        slots_x[bidx]["ev"] = ev_e
                    ev_pe = kb.op(
                        "tensor",
                        lambda E, o=slots_q[bidx]["t"][0:1, 0:tw],
                               r=qb[:, t0:t0 + tw], st=(k == 0), sp=(k == NCD - 1):
                            E.matmul(o, ones_c[:, 0:1], r, start=st, stop=sp),
                        waits=[ev_q] + ([slots_q[bidx]["ev"]] if k == 0 else []),
                        inc="pe")
                    if k == NCD - 1:
                        ev_e = vcopy(kb, rB0[0:1, t0:t0 + tw], slots_q[bidx]["t"][0:1, 0:tw],
                                     waits=[ev_pe])
                        slots_q[bidx]["ev"] = ev_e
                sq_war[k % 2] = ev_pe
                if main_w < W:
                    kb.drain("vector")
                    qr_evs[k] = kb.op("vector",
                                      lambda E, k=k, twr=W - main_w: E.tensor_mul(
                                          qrem[:, k * twr:(k + 1) * twr],
                                          A[k][:, main_w:W], A[k][:, main_w:W]),
                                      waits=[ld_ev[k]], inc="v")
            if main_w < W:
                t0r, twr = main_w, W - main_w
                s_x = pm.acquire()
                s_q = pm.acquire()
                for k in range(NCD):
                    kb.op("tensor",
                          lambda E, o=s_x["t"][0:1, 0:twr], r=A[k][:, t0r:t0r + twr],
                                 st=(k == 0), sp=(k == NCD - 1):
                              E.matmul(o, ones_c[:, 0:1], r, start=st, stop=sp),
                          waits=[s_x["ev"]] if k == 0 else [],
                          inc="pe" if k == NCD - 1 else None)
                    if k == NCD - 1:
                        ev_e = vcopy(kb, rA0[0:1, t0r:t0r + twr], s_x["t"][0:1, 0:twr],
                                     waits=[("pe", kb.cnt["pe"])])
                        s_x["ev"] = ev_e
                    ev_pe = kb.op(
                        "tensor",
                        lambda E, o=s_q["t"][0:1, 0:twr],
                               st=(k == 0), sp=(k == NCD - 1), k=k:
                            E.matmul(o, ones_c[:, 0:1], qrem[:, k * twr:(k + 1) * twr],
                                     start=st, stop=sp),
                        waits=[qr_evs[k]] + ([s_q["ev"]] if k == 0 else []),
                        inc="pe")
                    if k == NCD - 1:
                        ev_e = vcopy(kb, rB0[0:1, t0r:t0r + twr], s_q["t"][0:1, 0:twr],
                                     waits=[ev_pe])
                        s_q["ev"] = ev_e
            FF = next(f for f in range(max(1, W // 128), W + 1) if W % f == 0 and W // f <= 128)
            PP = W // FF
            ev_stS = kb.dma("gpsimd", "sw", dap(rows_d, 5 * W, [[1, W]]), rA0[0:1, 0:W],
                            waits=[("v", kb.cnt["v"])])
            ev_stQ = kb.dma("gpsimd", "sw", dap(rows_d, 6 * W, [[1, W]]), rB0[0:1, 0:W],
                            waits=[("v", kb.cnt["v"])])
            ev_ldS = kb.dma("gpsimd", "sw", pfS[0:PP, 0:FF],
                            dap(rows_d, 5 * W, [[FF, PP], [1, FF]]), waits=[ev_stS])
            ev_ldQ = kb.dma("gpsimd", "sw", pfQ[0:PP, 0:FF],
                            dap(rows_d, 6 * W, [[FF, PP], [1, FF]]), waits=[ev_stQ])
            kb.op("vector", lambda E: E.tensor_scalar(pfS[0:PP, 0:FF], pfS[0:PP, 0:FF],
                                                      1.0 / D_MODEL, None, ALU.mult),
                  waits=[ev_ldS])
            kb.op("vector", lambda E: E.tensor_scalar(pfQ[0:PP, 0:FF], pfQ[0:PP, 0:FF],
                                                      1.0 / D_MODEL, None, ALU.mult),
                  waits=[ev_ldQ])
            kb.drain("vector")
            ev_m1 = kb.op("vector", lambda E: E.tensor_mul(pfT[0:PP, 0:FF], pfS[0:PP, 0:FF],
                                                           pfS[0:PP, 0:FF]), inc="v")
            ev_stm = kb.dma("gpsimd", "sw", dap(rows_d, 0, [[FF, PP], [1, FF]]),
                            pfS[0:PP, 0:FF], waits=[ev_m1])
            kb.drain("vector")
            ev_var = kb.op("vector",
                           lambda E: E.tensor_sub(pfQ[0:PP, 0:FF], pfQ[0:PP, 0:FF], pfT[0:PP, 0:FF]),
                           inc="v")
            ev_s1 = kb.op("scalar", lambda E: E.activation(pfQ[0:PP, 0:FF], pfQ[0:PP, 0:FF], AF.Sqrt,
                                                           bias=epsc[0:PP, 0:1], scale=1.0),
                          waits=[ev_var, ev_eps], inc="sc")
            ev_sts = kb.dma("gpsimd", "sw", dap(rows_d, W, [[FF, PP], [1, FF]]),
                            pfQ[0:PP, 0:FF], waits=[ev_s1])
            ev_r1 = kb.op("vector", lambda E: E.reciprocal(pfT[0:PP, 0:FF], pfQ[0:PP, 0:FF]),
                          waits=[ev_s1], inc="v")
            ev_str = kb.dma("gpsimd", "sw", dap(rows_d, 4 * W, [[FF, PP], [1, FF]]),
                            pfT[0:PP, 0:FF], waits=[ev_r1])
            ev_ba = kb.dma("gpsimd", "sw", ba0[:, :], dap(rows_d, 0, [[0, 128], [1, W]]),
                           waits=[ev_stm])
            ev_bb = kb.dma("gpsimd", "sw", bb_0[:, :], dap(rows_d, 4 * W, [[0, 128], [1, W]]),
                           waits=[ev_str])
            kb.end("p0")

        with ExitStack() as p1:
            wo = [p1.enter_context(nc.sbuf_tensor(f"wo{i}", [128, GK * 128], BF16)) for i in range(2)]
            m1r = p1.enter_context(nc.sbuf_tensor("m1r", [1, ntok], BF16))
            xi_sb = p1.enter_context(nc.sbuf_tensor("xi_sb", [128, W], BF16))
            cacc = p1.enter_context(nc.sbuf_tensor("cacc", [128, ntok], BF16))
            sg_m = p1.enter_context(nc.sbuf_tensor("sg_m", [128, ntok], BF16))
            H2 = [ba0[:, 0:ntok], bb_0[:, 0:ntok], s1bt[:, 0:ntok]] + [
                p1.enter_context(nc.sbuf_tensor(f"H2_{i}", [128, ntok], BF16))
                for i in range(GK - 3)]

            kb.begin()
            kb.wait("vector", ev_cwk)
            kb.wait("scalar", ev_cb)
            kb.wait("vector", ev_dd)
            app_ev = [None] * NCD
            for c in range(NCD):
                kb.drain("vector")
                kb.op("vector",
                      lambda E, c=c: E.tensor_sub(A[c][:, :], A[c][:, :], ba0[:, :]))
                kb.drain("vector")
                app_ev[c] = kb.op("vector",
                                  lambda E, c=c: E.tensor_mul(A[c][:, :], A[c][:, :],
                                                              bb_0[:, :]),
                                  inc="v")
            ev_s1bt = kb.dma("gpsimd", "sw", s1bt[:, 0:ntok],
                             dap(rows_d, W + TC0, [[0, 128], [1, ntok]]), waits=[ev_sts])
            ev_m1r = kb.dma("gpsimd", "sw", m1r[0:1, 0:ntok],
                            dap(rows_d, TC0, [[1, ntok]]), waits=[ev_stm])
            binit_ev = [None] * NCD
            for c in range(NCD):
                kb.drain("vector")
                binit_ev[c] = kb.op("vector",
                                    lambda E, c=c: E.tensor_mul(B[c][:, :],
                                                                A[c][:, TC0:TC0 + ntok],
                                                                s1bt[:, 0:ntok]),
                                    waits=[app_ev[c], ev_s1bt], inc="v")
            wxi_cons = [None, None]
            wga_cons = [None, None]
            wo_cons = [None, None]
            b_fin = [None] * NCD
            h2_ev = [None] * GK
            h2_gate = None
            wo_pf = {}
            ev_sxc = None
            ev_h2_prev = None
            for g in range(NG):
                for dl in range(GK):
                    if dl == GK - 1:
                        for co_p in (0, 1):
                            wo_pf[co_p] = kb.dma(
                                "gpsimd", "sw",
                                wo[co_p % 2][:, :].rearrange("p (k j) -> p k j", j=128),
                                dap(w_outT, (g * GK * 128) * D_MODEL + co_p * 128,
                                    [[D_MODEL, 128], [128 * D_MODEL, GK], [1, 128]]),
                                waits=[wo_cons[co_p % 2]])
                    dc = g * GK + dl
                    bi = dc % 2
                    if dc == 0:
                        ld_xi, ld_ga = pf_xi, pf_ga
                    else:
                        ld_xi = kb.dma(
                            "gpsimd", "sw",
                            wxi[bi][:, :].rearrange("p (k j) -> p k j", j=128),
                            dap(w_inT, dc * 128,
                                [[2 * D_INNER, 128], [128 * 2 * D_INNER, NCD], [1, 128]]),
                            waits=[wxi_cons[bi]])
                        ld_ga = kb.dma(
                            "gpsimd", "sw",
                            wga[bi][:, :].rearrange("p (k j) -> p k j", j=128),
                            dap(w_inT, D_INNER + dc * 128,
                                [[2 * D_INNER, 128], [128 * 2 * D_INNER, NCD], [1, 128]]),
                            waits=[wga_cons[bi]])
                    xi_evs = []
                    last_pe = None
                    for (t0, tw) in tblks(W):
                        slot = pm.acquire()
                        pairs = [(wxi[bi][:, k * 128:(k + 1) * 128], A[k][:, t0:t0 + tw],
                                  [ld_xi, app_ev[k]]) for k in range(NCD)]
                        ev_pe = mm_group(kb, slot["t"][:, 0:tw], pairs, waits_first=[slot["ev"]])
                        ev_e = vcopy(kb, xi_sb[:, t0:t0 + tw], slot["t"][:, 0:tw], waits=[ev_pe])
                        slot["ev"] = ev_e
                        xi_evs.append(ev_e)
                        last_pe = ev_pe
                    wxi_cons[bi] = last_pe
                    sg_evs = []
                    for (t0, tw) in tblks(ntok):
                        slot = pm.acquire()
                        pairs = [(wga[bi][:, k * 128:(k + 1) * 128],
                                  A[k][:, TC0 + t0:TC0 + t0 + tw], [ld_ga, app_ev[k]])
                                 for k in range(NCD)]
                        ev_pe = mm_group(kb, slot["t"][:, 0:tw], pairs, waits_first=[slot["ev"]])
                        ev_e = kb.op("scalar",
                                     lambda E, d=sg_m[:, t0:t0 + tw], s=slot["t"][:, 0:tw]:
                                         E.activation(d, s, AF.Silu),
                                     waits=[ev_pe, ev_h2_prev], inc="sc")
                        slot["ev"] = ev_e
                        sg_evs.append(ev_e)
                        last_pe = ev_pe
                    wga_cons[bi] = last_pe
                    kb.drain("vector")
                    kb.op("vector",
                          lambda E, dc=dc: E.tensor_single_scalar(
                              cacc[:, 0:ntok], xi_sb[:, 1:1 + ntok],
                              cwk[:, dc * 4:dc * 4 + 1], ALU.mult),
                          waits=[ev_sxc])
                    ev_conv = None
                    for kk in range(1, 4):
                        kb.drain("vector")
                        ev_conv = kb.op("vector",
                                        lambda E, dc=dc, kk=kk: E.scalar_tensor_tensor(
                                            cacc[:, 0:ntok], xi_sb[:, 1 + kk:1 + kk + ntok],
                                            cwk[:, dc * 4 + kk:dc * 4 + kk + 1], cacc[:, 0:ntok],
                                            ALU.mult, ALU.add),
                                        inc="v" if kk == 3 else None)
                    ev_sxc = kb.op("scalar",
                                   lambda E, dc=dc: E.activation(cacc[:, 0:ntok], cacc[:, 0:ntok],
                                                                 AF.Silu,
                                                                 bias=cbc[:, dc:dc + 1], scale=1.0),
                                   waits=[ev_conv, ev_h2_prev], inc="sc")
                    kb.drain("vector")
                    ev_h2 = kb.op("vector",
                                  lambda E, dl=dl, dc=dc: E.scalar_tensor_tensor(
                                      H2[dl][:, 0:ntok], cacc[:, 0:ntok], ddc[:, dc:dc + 1],
                                      sg_m[:, 0:ntok], ALU.mult, ALU.mult),
                                  waits=[ev_sxc, h2_gate] + sg_evs, inc="v")
                    h2_ev[dl] = ev_h2
                    ev_h2_prev = ev_h2
                kb.drain("vector")
                last_pe = None
                for co in range(NCD):
                    bi = co % 2
                    if co in wo_pf:
                        ld_wo = wo_pf.pop(co)
                    else:
                        ld_wo = kb.dma(
                            "gpsimd", "sw",
                            wo[bi][:, :].rearrange("p (k j) -> p k j", j=128),
                            dap(w_outT, (g * GK * 128) * D_MODEL + co * 128,
                                [[D_MODEL, 128], [128 * D_MODEL, GK], [1, 128]]),
                            waits=[wo_cons[bi]])
                    for (t0, tw) in tblks(ntok):
                        slot = pm.acquire()
                        pairs = [(wo[bi][:, kk * 128:(kk + 1) * 128], H2[kk][:, t0:t0 + tw],
                                  [ld_wo, h2_ev[kk]]) for kk in range(GK)]
                        if g == 0:
                            pairs.append((ones_r[0:1, 0:128], m1r[0:1, t0:t0 + tw],
                                          [ev_m1r, ev_ones_r]))
                        ev_pe = mm_group(kb, slot["t"][:, 0:tw], pairs, waits_first=[slot["ev"]])
                        ev_e = kb.op("vector",
                                     lambda E, d=B[co][:, t0:t0 + tw], s=slot["t"][:, 0:tw]:
                                         E.tensor_add(d, d, s),
                                     waits=[ev_pe, binit_ev[co]], inc="v")
                        slot["ev"] = ev_e
                        last_pe = ev_pe
                        if g == NG - 1:
                            b_fin[co] = ev_e
                    wo_cons[bi] = last_pe
                h2_gate = last_pe
            kb.end("p1")
        p01.close()

        with ExitStack() as p2:
            bb0 = p2.enter_context(nc.sbuf_tensor("bb0", [128, ntok], BF16))
            bb1 = p2.enter_context(nc.sbuf_tensor("bb1", [128, ntok], BF16))
            trt = p2.enter_context(nc.sbuf_tensor("trt", [128, ntok], BF16))
            sqb = [p2.enter_context(nc.sbuf_tensor(f"sqb{i}", [128, ntok], BF16)) for i in range(2)]
            rowA = p2.enter_context(nc.sbuf_tensor("rowA", [1, ntok], F32))
            rowB = p2.enter_context(nc.sbuf_tensor("rowB", [1, ntok], F32))
            pf2S = p2.enter_context(nc.sbuf_tensor("pf2S", [128, 16], F32))
            pf2Q = p2.enter_context(nc.sbuf_tensor("pf2Q", [128, 16], F32))
            pf2T = p2.enter_context(nc.sbuf_tensor("pf2T", [128, 16], F32))
            gwsb = p2.enter_context(nc.sbuf_tensor("gwsb", [128, NCD * NE], BF16))
            cwf8 = p2.enter_context(nc.sbuf_tensor("cwf8s", [NE, ntok], BF16))
            lg = p2.enter_context(nc.sbuf_tensor("lg", [NE, ntok], F32))
            ltw = p2.enter_context(nc.sbuf_tensor("ltw", [128, NT * NE], F32))
            msw = p2.enter_context(nc.sbuf_tensor("msw", [128, NT * NE], F32))
            ctw = p2.enter_context(nc.sbuf_tensor("ctw", [128, NT * NE], F32))
            gs = p2.enter_context(nc.sbuf_tensor("gs", [128, 3 * NT], F32))

            kb.begin()
            ev_gw = kb.dma("gpsimd", "sw",
                           gwsb[:, :].rearrange("p (k e) -> p k e", e=NE),
                           dap(gwT_d, 0, [[NE, 128], [128 * NE, NCD], [1, NE]]))
            pf_s1 = kb.dma("gpsimd", "sw",
                           wxi[0][:, :].rearrange("p (k j) -> p k j", j=128),
                           dap(sw1T, 0, [[I_FFN, 128], [128 * I_FFN, NCD], [1, 128]]))
            pf_s2 = kb.dma("gpsimd", "sw",
                           wga[0][:, :].rearrange("p (k j) -> p k j", j=128),
                           dap(sw2T, 0, [[I_FFN, 128], [128 * I_FFN, NCD], [1, 128]]))

            def bcast(row_idx, dst, waits=()):
                return [kb.dma("gpsimd", "sw", dst[:, :],
                               dap(rows_d, row_idx * W + TC0, [[0, 128], [1, ntok]]),
                               waits=list(waits))]

            recon_ev = b_fin
            kb.drain("vector")
            blks = tblks(ntok)
            nblk = len(blks)
            slots_x = [pm.acquire() for _ in range(nblk)]
            slots_q = [pm.acquire() for _ in range(nblk)]
            war_sq = [None, None]
            for k in range(NCD):
                qb = sqb[k % 2]
                if k % 2 == 0:
                    ev_q = kb.op("scalar",
                                 lambda E, k=k, qb=qb: E.activation(qb[:, :], B[k][:, :],
                                                                    AF.Square),
                                 waits=[recon_ev[k], war_sq[k % 2]], inc="sc")
                else:
                    kb.drain("vector")
                    ev_q = kb.op("vector",
                                 lambda E, k=k, qb=qb: E.tensor_mul(qb[:, :], B[k][:, :],
                                                                    B[k][:, :]),
                                 waits=[recon_ev[k], war_sq[k % 2]], inc="v")
                for bidx, (t0, tw) in enumerate(blks):
                    kb.op("tensor",
                          lambda E, o=slots_x[bidx]["t"][0:1, 0:tw],
                                 r=B[k][:, t0:t0 + tw], st=(k == 0), sp=(k == NCD - 1):
                              E.matmul(o, ones_c[:, 0:1], r, start=st, stop=sp),
                          waits=[recon_ev[k]] + ([slots_x[bidx]["ev"]] if k == 0 else []),
                          inc="pe" if k == NCD - 1 else None)
                    if k == NCD - 1:
                        ev_e = vcopy(kb, rowA[0:1, t0:t0 + tw], slots_x[bidx]["t"][0:1, 0:tw],
                                     waits=[("pe", kb.cnt["pe"])])
                        slots_x[bidx]["ev"] = ev_e
                    ev_pe = kb.op(
                        "tensor",
                        lambda E, o=slots_q[bidx]["t"][0:1, 0:tw],
                               r=qb[:, t0:t0 + tw], st=(k == 0), sp=(k == NCD - 1):
                            E.matmul(o, ones_c[:, 0:1], r, start=st, stop=sp),
                        waits=[ev_q] + ([slots_q[bidx]["ev"]] if k == 0 else []),
                        inc="pe")
                    if k == NCD - 1:
                        ev_e = vcopy(kb, rowB[0:1, t0:t0 + tw], slots_q[bidx]["t"][0:1, 0:tw],
                                     waits=[ev_pe])
                        slots_q[bidx]["ev"] = ev_e
                war_sq[k % 2] = ev_pe
            ev_stS2 = kb.dma("gpsimd", "sw", dap(rows_d, 5 * W, [[1, ntok]]), rowA[0:1, :],
                             waits=[("v", kb.cnt["v"])])
            ev_stQ2 = kb.dma("gpsimd", "sw", dap(rows_d, 6 * W, [[1, ntok]]), rowB[0:1, :],
                             waits=[("v", kb.cnt["v"])])
            ev_ldS2 = kb.dma("gpsimd", "sw", pf2S[0:128, 0:NT],
                             dap(rows_d, 5 * W, [[NT, 128], [1, NT]]), waits=[ev_stS2])
            ev_ldQ2 = kb.dma("gpsimd", "sw", pf2Q[0:128, 0:NT],
                             dap(rows_d, 6 * W, [[NT, 128], [1, NT]]), waits=[ev_stQ2])
            kb.op("vector", lambda E: E.tensor_scalar(pf2S[:, 0:NT], pf2S[:, 0:NT],
                                                      1.0 / D_MODEL, None, ALU.mult),
                  waits=[ev_ldS2])
            kb.op("vector", lambda E: E.tensor_scalar(pf2Q[:, 0:NT], pf2Q[:, 0:NT],
                                                      1.0 / D_MODEL, None, ALU.mult),
                  waits=[ev_ldQ2])
            kb.drain("vector")
            ev_m2 = kb.op("vector", lambda E: E.tensor_mul(pf2T[:, 0:NT], pf2S[:, 0:NT],
                                                           pf2S[:, 0:NT]), inc="v")
            ev_st_m2 = kb.dma("gpsimd", "sw", dap(rows_d, 2 * W + TC0, [[NT, 128], [1, NT]]),
                              pf2S[0:128, 0:NT], waits=[ev_m2])
            kb.drain("vector")
            ev_var = kb.op("vector",
                           lambda E: E.tensor_sub(pf2Q[:, 0:NT], pf2Q[:, 0:NT], pf2T[:, 0:NT]),
                           inc="v")
            ev_sr = kb.op("scalar", lambda E: E.activation(pf2Q[:, 0:NT], pf2Q[:, 0:NT], AF.Sqrt,
                                                           bias=epsc[0:128, 0:1], scale=1.0),
                          waits=[ev_var], inc="sc")
            ev_r2 = kb.op("vector", lambda E: E.reciprocal(pf2T[:, 0:NT], pf2Q[:, 0:NT]),
                          waits=[ev_sr], inc="v")
            ev_st_r2 = kb.dma("gpsimd", "sw", dap(rows_d, 3 * W + TC0, [[NT, 128], [1, NT]]),
                              pf2T[0:128, 0:NT], waits=[ev_r2])
            ev_m2b = bcast(2, bb1, waits=[ev_st_m2])
            ev_r2b = bcast(3, bb0, waits=[ev_st_r2])
            xn2_ev = [None] * NCD
            for c in range(NCD):
                kb.drain("vector")
                kb.op("vector", lambda E, c=c: E.tensor_sub(trt[:, :], B[c][:, :], bb1[:, :]),
                      waits=ev_m2b)
                kb.drain("vector")
                xn2_ev[c] = kb.op("vector",
                                  lambda E, c=c: E.tensor_mul(A[c][:, TC0:TC0 + ntok],
                                                              trt[:, :], bb0[:, :]),
                                  waits=ev_r2b, inc="v")
            lg_evs = []
            for (t0, tw) in tblks(ntok):
                slot = pm.acquire()
                pairs = [(gwsb[:, k * NE:(k + 1) * NE], A[k][:, TC0 + t0:TC0 + t0 + tw],
                          [ev_gw, xn2_ev[k]]) for k in range(NCD)]
                ev_pe = mm_group(kb, slot["t"][0:NE, 0:tw], pairs, waits_first=[slot["ev"]])
                ev_e = vcopy(kb, lg[0:NE, t0:t0 + tw], slot["t"][0:NE, 0:tw], waits=[ev_pe])
                slot["ev"] = ev_e
                lg_evs.append(ev_e)
            ev_ltw = None
            for i in range(NT):
                s_t = pm.acquire()
                ev_t = transp(kb, s_t["t"][:, 0:NE],
                              lg[0:NE, i * 128:(i + 1) * 128], identf[0:NE, 0:NE],
                              waits=[s_t["ev"], lg_evs[(i * 128) // 512]])
                ev_ltw = vcopy(kb, ltw[:, i * NE:(i + 1) * NE], s_t["t"][:, 0:NE], waits=[ev_t])
                s_t["ev"] = ev_ltw

            def r3(t):
                return t[:, 0:NT * NE].rearrange("p (i e) -> p i e", e=NE)

            def b3(gslice):
                return gslice.unsqueeze(2).broadcast_to([128, NT, NE])

            def vop(fn, waits=(), inc=None, dr=True):
                if dr:
                    kb.drain("vector")
                return kb.op("vector", fn, waits=waits, inc=inc)

            g0, g1, g2 = gs[:, 0:NT], gs[:, NT:2 * NT], gs[:, 2 * NT:3 * NT]
            vop(lambda E: E.reduce_max(g0, r3(ltw), axis=AX.X), waits=[ev_ltw])
            ev_sh = vop(lambda E: E.tensor_sub(r3(ltw), r3(ltw), b3(g0)), inc="v")
            ev_ex = kb.op("scalar", lambda E: E.activation(ltw[:, 0:NT * NE], ltw[:, 0:NT * NE],
                                                           AF.Exp),
                          waits=[ev_sh], inc="sc")
            vop(lambda E: E.reduce_sum(g1, r3(ltw), axis=AX.X), waits=[ev_ex], dr=False)
            vop(lambda E: E.reciprocal(g2, g1))
            ev_p = vop(lambda E: E.tensor_mul(r3(ltw), r3(ltw), b3(g2)), inc="v")
            vop(lambda E: E.reduce_max(g0, r3(ltw), axis=AX.X))
            vop(lambda E: E.tensor_tensor(r3(msw), r3(ltw), b3(g0), ALU.is_equal))
            vop(lambda E: E.scalar_tensor_tensor(ctw[:, 0:NT * NE], msw[:, 0:NT * NE], -1e30,
                                                 ltw[:, 0:NT * NE], ALU.mult, ALU.add))
            vop(lambda E: E.reduce_max(g1, r3(ctw), axis=AX.X))
            vop(lambda E: E.tensor_tensor(r3(ctw), r3(ctw), b3(g1), ALU.is_equal))
            vop(lambda E: E.tensor_add(g2, g0, g1))
            vop(lambda E: E.tensor_scalar(g2, g2, 1e-20, None, ALU.add))
            vop(lambda E: E.reciprocal(g2, g2))
            vop(lambda E: E.tensor_mul(g0, g0, g2))
            kb.op("vector", lambda E: E.tensor_mul(g1, g1, g2))
            vop(lambda E: E.tensor_mul(r3(msw), r3(msw), b3(g0)))
            kb.op("vector", lambda E: E.tensor_mul(r3(ctw), r3(ctw), b3(g1)))
            ev_cw = vop(lambda E: E.tensor_add(msw[:, 0:NT * NE], msw[:, 0:NT * NE],
                                               ctw[:, 0:NT * NE]), inc="v")
            for i in range(NT):
                s_b = pm.acquire()
                ev_t2 = transp(kb, s_b["t"][0:NE, 0:128], msw[:, i * NE:(i + 1) * NE],
                               identf[0:128, 0:128], waits=[s_b["ev"], ev_cw])
                ev_e2 = vcopy(kb, cwf8[0:NE, i * 128:(i + 1) * 128], s_b["t"][0:NE, 0:128],
                              waits=[ev_t2])
                s_b["ev"] = ev_e2
            ev_cwf = kb.dma("gpsimd", "sw", rows8_d[0:NE, 0:ntok], cwf8[0:NE, 0:ntok],
                            waits=[("v", kb.cnt["v"])])
            kb.end("p2")

        with ExitStack() as p3:
            P = [p3.enter_context(nc.sbuf_tensor(f"P{i}", [128, ntok], BF16)) for i in range(NCI)]
            sgE = [p3.enter_context(nc.sbuf_tensor(f"sgE{i}", [128, ntok], BF16)) for i in range(2)]
            cwb = p3.enter_context(nc.sbuf_tensor("cwb", [128, ntok], BF16))
            w2b = [p3.enter_context(nc.sbuf_tensor(f"w2b_{i}", [128, NCI * 128], BF16)) for i in range(2)]

            kb.begin()
            wg_cons = [None, None]
            wu_cons = [None, None]
            w2_cons = [None, None]
            sg_war = [None, None]
            st_war = [None, None]
            ev_he_last = None
            ev_sg = None
            ev_pmul_prev = None
            for ic in range(NCI if 'shared' not in dbg_skip else 0):
                bi = ic % 2
                if ic == 0:
                    ld1, ld2 = pf_s1, pf_s2
                else:
                    ld1 = kb.dma("gpsimd", "sw",
                                 wxi[bi][:, :].rearrange("p (k j) -> p k j", j=128),
                                 dap(sw1T, ic * 128, [[I_FFN, 128], [128 * I_FFN, NCD], [1, 128]]),
                                 waits=[wg_cons[bi]])
                    ld2 = kb.dma("gpsimd", "sw",
                                 wga[bi][:, :].rearrange("p (k j) -> p k j", j=128),
                                 dap(sw2T, ic * 128, [[I_FFN, 128], [128 * I_FFN, NCD], [1, 128]]),
                                 waits=[wu_cons[bi]])
                last_pe = None
                for (t0, tw) in tblks(ntok):
                    slot = pm.acquire()
                    pairs = [(wxi[bi][:, k * 128:(k + 1) * 128],
                              A[k][:, TC0 + t0:TC0 + t0 + tw], [ld1]) for k in range(NCD)]
                    ev_pe = mm_group(kb, slot["t"][:, 0:tw], pairs, waits_first=[slot["ev"]])
                    sgb = sgE[ic % 2]
                    ev_e = kb.op("scalar",
                                 lambda E, d=sgb[:, t0:t0 + tw], s=slot["t"][:, 0:tw]:
                                     E.activation(d, s, AF.Silu),
                                 waits=[ev_pe, sg_war[ic % 2]], inc="sc")
                    slot["ev"] = ev_e
                    ev_sg = ev_e
                    last_pe = ev_pe
                wg_cons[bi] = last_pe
                for (t0, tw) in tblks(ntok):
                    slot = pm.acquire()
                    pairs = [(wga[bi][:, k * 128:(k + 1) * 128],
                              A[k][:, TC0 + t0:TC0 + t0 + tw], [ld2]) for k in range(NCD)]
                    ev_pe = mm_group(kb, slot["t"][:, 0:tw], pairs, waits_first=[slot["ev"]])
                    ev_e = vcopy(kb, P[ic][:, t0:t0 + tw], slot["t"][:, 0:tw], waits=[ev_pe])
                    slot["ev"] = ev_e
                    last_pe = ev_pe
                wu_cons[bi] = last_pe
                kb.drain("vector")
                ev_pmul_prev = kb.op("vector",
                                     lambda E, ic=ic, sgb=sgE[ic % 2]: E.tensor_mul(
                                         P[ic][:, :], P[ic][:, :], sgb[:, :]),
                                     waits=[ev_sg], inc="v")
                sg_war[ic % 2] = ev_pmul_prev
            p_gate = None
            kb.drain("vector")
            for co in range(NCD):
                bi = co % 2
                ld3 = kb.dma("gpsimd", "sw",
                             w2b[bi][:, :].rearrange("p (k j) -> p k j", j=128),
                             dap(sw3T, co * 128, [[D_MODEL, 128], [128 * D_MODEL, NCI], [1, 128]]),
                             waits=[w2_cons[bi]])
                for (t0, tw) in tblks(ntok):
                    slot = pm.acquire()
                    pairs = [(w2b[bi][:, k * 128:(k + 1) * 128], P[k][:, t0:t0 + tw],
                              [ld3, ev_pmul_prev]) for k in range(NCI)]
                    ev_pe = mm_group(kb, slot["t"][:, 0:tw], pairs, waits_first=[slot["ev"]])
                    ev_e = kb.op("vector",
                                 lambda E, d=B[co][:, t0:t0 + tw], s=slot["t"][:, 0:tw]:
                                     E.tensor_add(d, d, s),
                                 waits=[ev_pe], inc="v")
                    slot["ev"] = ev_e
                    p_gate = ev_pe
                w2_cons[bi] = p_gate
            ev_sg = None
            ne_run = NE
            for d in dbg_skip:
                if d.startswith('ne='):
                    ne_run = int(d[3:])
            if 'experts' in dbg_skip:
                ne_run = 0
            for e in range(ne_run):
                kb.drain("vector")
                cwb_evs = [kb.dma("gpsimd", "sw", cwb[:, :],
                                  dap(rows8_d, e * ntok, [[0, 128], [1, ntok]]),
                                  waits=[ev_cwf, ev_pmul_prev])]
                p_evs = [None] * NCI
                for J in range(NCI):
                    bi = J % 2
                    ldg = kb.dma("gpsimd", "sw",
                                 wxi[bi][:, :].rearrange("p (k j) -> p k j", j=128),
                                 dap(w1_d, e * D_MODEL * TWOI + J * 128,
                                     [[TWOI, 128], [128 * TWOI, NCD], [1, 128]]),
                                 waits=[wg_cons[bi]])
                    ldu = kb.dma("gpsimd", "sw",
                                 wga[bi][:, :].rearrange("p (k j) -> p k j", j=128),
                                 dap(w1_d, e * D_MODEL * TWOI + I_FFN + J * 128,
                                     [[TWOI, 128], [128 * TWOI, NCD], [1, 128]]),
                                 waits=[wu_cons[bi]])
                    last_pe = None
                    for (t0, tw) in tblks(ntok):
                        slot = pm.acquire()
                        pairs = [(wxi[bi][:, k * 128:(k + 1) * 128],
                                  A[k][:, TC0 + t0:TC0 + t0 + tw], [ldg]) for k in range(NCD)]
                        ev_pe = mm_group(kb, slot["t"][:, 0:tw], pairs, waits_first=[slot["ev"]])
                        sgb = sgE[J % 2]
                        ev_e = kb.op("scalar",
                                     lambda E, d=sgb[:, t0:t0 + tw], s=slot["t"][:, 0:tw]:
                                         E.activation(d, s, AF.Silu),
                                     waits=[ev_pe, sg_war[J % 2]], inc="sc")
                        slot["ev"] = ev_e
                        ev_sg = ev_e
                        last_pe = ev_pe
                    wg_cons[bi] = last_pe
                    for (t0, tw) in tblks(ntok):
                        slot = pm.acquire()
                        pairs = [(wga[bi][:, k * 128:(k + 1) * 128],
                                  A[k][:, TC0 + t0:TC0 + t0 + tw], [ldu]) for k in range(NCD)]
                        ev_pe = mm_group(kb, slot["t"][:, 0:tw], pairs, waits_first=[slot["ev"]])
                        ev_e = vcopy(kb, P[J][:, t0:t0 + tw], slot["t"][:, 0:tw],
                                     waits=[ev_pe, p_gate])
                        slot["ev"] = ev_e
                        last_pe = ev_pe
                    wu_cons[bi] = last_pe
                    ev_he_last = last_pe
                    kb.drain("vector")
                    kb.op("vector", lambda E, J=J, sgb=sgE[J % 2]: E.tensor_mul(
                              P[J][:, :], P[J][:, :], sgb[:, :]),
                          waits=[ev_sg], inc="v")
                    sg_war[J % 2] = (kb.cnt["v"] and ("v", kb.cnt["v"]))
                    kb.drain("vector")
                    ev_pm = kb.op("vector",
                                  lambda E, J=J: E.tensor_mul(P[J][:, :], P[J][:, :], cwb[:, :]),
                                  waits=cwb_evs, inc="v")
                    p_evs[J] = ev_pm
                    ev_pmul_prev = ev_pm
                kb.drain("vector")
                for cb_i in range(NCD):
                    bi = cb_i % 2
                    ld2 = kb.dma("gpsimd", "sw",
                                 w2b[bi][:, :].rearrange("p (k j) -> p k j", j=128),
                                 dap(w2_d, e * I_FFN * D_MODEL + cb_i * 128,
                                     [[D_MODEL, 128], [128 * D_MODEL, NCI], [1, 128]]),
                                 waits=[w2_cons[bi]])
                    for (t0, tw) in tblks(ntok):
                        slot = pm.acquire()
                        pairs = [(w2b[bi][:, k * 128:(k + 1) * 128], P[k][:, t0:t0 + tw],
                                  [ld2, p_evs[k]]) for k in range(NCI)]
                        ev_pe = mm_group(kb, slot["t"][:, 0:tw], pairs, waits_first=[slot["ev"]])
                        ev_e = kb.op("vector",
                                     lambda E, d=B[cb_i][:, t0:t0 + tw], s=slot["t"][:, 0:tw]:
                                         E.tensor_add(d, d, s),
                                     waits=[ev_pe], inc="v")
                        slot["ev"] = ev_e
                        p_gate = ev_pe
                    w2_cons[bi] = p_gate
                    if e == ne_run - 1:
                        view = A[cb_i][:, :].bitcast(F32)
                        fw = view.shape[1]
                        off = 0
                        while off < ntok:
                            w = min(fw, ntok - off)
                            kb.drain("vector")
                            ev_cast = kb.op(
                                "vector",
                                lambda E, d=view[:, 0:w], s=B[cb_i][:, off:off + w]:
                                    E.tensor_copy(d, s),
                                waits=[ev_e, ev_he_last, st_war[cb_i % 2]], inc="v")
                            st_war[cb_i % 2] = kb.dma(
                                "sync", "sst", y_d[cb_i * 128:(cb_i + 1) * 128, off:off + w],
                                view[:, 0:w], waits=[ev_cast])
                            off += w
            kb.end("p3")

        if ne_run != NE:
            kb.begin()
            for c in range(NCD):
                kb.dma("gpsimd", "sw", y_d[c * 128:(c + 1) * 128, :], B[c][:, :])
            kb.end("p4")

    return nc


def _stage(inputs):
    x = np.ascontiguousarray(np.asarray(inputs["L0_attn_x"], dtype=np.float32))
    w_in = np.asarray(inputs["L0_attn_w_in"], dtype=np.float32)
    w_out = np.asarray(inputs["L0_attn_w_out"], dtype=np.float32)
    conv_w = np.asarray(inputs["L0_attn_conv_w"], dtype=np.float32)
    conv_b = np.asarray(inputs["L0_attn_conv_b"], dtype=np.float32)
    dvec = np.asarray(inputs["L0_attn_D"], dtype=np.float32)
    gate_w = np.asarray(inputs["L0_ffn_gate_w"], dtype=np.float32)
    w1 = np.ascontiguousarray(np.asarray(inputs["L0_ffn_w1"], dtype=np.float32))
    w2 = np.ascontiguousarray(np.asarray(inputs["L0_ffn_w2"], dtype=np.float32))
    sw1 = np.asarray(inputs["L0_ffn_sw1"], dtype=np.float32)
    sw2 = np.asarray(inputs["L0_ffn_sw2"], dtype=np.float32)
    sw3 = np.asarray(inputs["L0_ffn_sw3"], dtype=np.float32)

    shared = {
        "w_inT": np.ascontiguousarray(w_in.T),
        "w_outT": np.ascontiguousarray(w_out.T),
        "w1": w1,
        "w2": w2,
        "sw1T": np.ascontiguousarray(sw1.T),
        "sw2T": np.ascontiguousarray(sw2.T),
        "sw3T": np.ascontiguousarray(sw3.T),
        "gwT": np.ascontiguousarray(gate_w.T),
        "cwk": np.ascontiguousarray(
            conv_w[:, 0, :].reshape(32, 128, 4).transpose(1, 0, 2).reshape(128, 128)),
        "cbcol": np.ascontiguousarray(conv_b.reshape(32, 128).T),
        "dcol": np.ascontiguousarray(dvec.reshape(32, 128).T),
        "identf": np.eye(128, dtype=np.float32),
        "identb": np.eye(128, dtype=ml_dtypes.bfloat16),
    }
    in_maps = []
    bsz, tfull, _ = x.shape
    half = tfull // 2
    for core in range(8):
        b, h = core // 2, core % 2
        t0 = h * half
        arr = np.zeros((half + 4, D_MODEL), np.float32)
        arr[4:] = x[b, t0:t0 + half]
        if t0 > 0:
            arr[1:4] = x[b, t0 - 3:t0]
        in_maps.append(dict(shared, xfm=np.ascontiguousarray(arr.T)))
    return in_maps


def kernel(**inputs):
    in_maps = _stage(inputs)
    ntok = in_maps[0]["xfm"].shape[1] - TC0
    nc = build_nc(ntok)
    res = run_bass_kernel_spmd(nc, in_maps, core_ids=list(range(8)))
    x = np.asarray(inputs["L0_attn_x"])
    bsz, tfull, dm = x.shape
    half = tfull // 2
    out = np.empty((bsz, tfull, dm), np.float32)
    for core, r in enumerate(res.results):
        b, h = core // 2, core % 2
        out[b, h * half:(h + 1) * half, :] = r["y"].T
    return out


# version 16
# speedup vs baseline: 1.0105x; 1.0002x over previous
import numpy as np
import ml_dtypes
from contextlib import ExitStack

import concourse.bass as bass
import concourse.mybir as mybir
from concourse.bass_utils import run_bass_kernel_spmd

F32 = mybir.dt.float32
BF16 = mybir.dt.bfloat16
AF = mybir.ActivationFunctionType
ALU = mybir.AluOpType
AX = mybir.AxisListType

D_MODEL = 2048
NCD = 16
D_INNER = 4096
NCDI = 32
I_FFN = 1408
NCI = 11
TWOI = 2816
NE = 8
EPS = 1e-5
GK = 8
NG = NCDI // GK
TC0 = 4

ENGS = ("tensor", "vector", "scalar", "sync", "gpsimd")


def tblks(n, w=512):
    return [(s, min(w, n - s)) for s in range(0, n, w)]


def dap(t, off, dims):
    return bass.AP(t, off, [list(d) for d in dims])


QUEUES = {"sw": 6, "sx": 3, "sst": 2}


class KB:

    def __init__(self, nc):
        self.nc = nc
        self.sems = {}
        self.cnt = {}
        self.q = None
        self.waited = {e: {} for e in ENGS}
        self.qlanes = {}
        self.qrr = {}

    def add_sem(self, stack, name):
        self.sems[name] = stack.enter_context(self.nc.semaphore(name))
        self.cnt[name] = 0

    def add_queue(self, stack, qname, nlanes):
        self.qlanes[qname] = []
        self.qrr[qname] = 0
        for i in range(nlanes):
            n = f"{qname}{i}"
            self.add_sem(stack, n)
            self.qlanes[qname].append(n)

    def begin(self):
        self.q = {e: [] for e in ENGS}

    def end(self, name):
        for qn, eng in (("sw", "gpsimd"), ("sx", "sync"), ("sst", "sync")):
            for lane in self.qlanes.get(qn, ()):
                self.wait(eng, (lane, self.cnt[lane]))
        q = self.q
        with self.nc.Block(name) as blk:
            for e in ENGS:
                lst = q[e]
                if lst:
                    def body(E, lst=tuple(lst)):
                        for f in lst:
                            f(E)
                    getattr(blk, e)(body)
        for e in ENGS:
            for s, v in self.cnt.items():
                self.waited[e][s] = max(self.waited[e].get(s, 0), v)
        self.q = None

    def wait(self, eng, ev):
        if not ev:
            return
        s, v = ev
        if self.waited[eng].get(s, 0) >= v:
            return
        self.waited[eng][s] = v
        self.q[eng].append(lambda E, h=self.sems[s], v=v: E.wait_ge(h, v))

    def op(self, eng, fn, waits=(), inc=None):
        for ev in waits:
            self.wait(eng, ev)
        if inc is None:
            self.q[eng].append(fn)
            return None
        self.cnt[inc] += 1
        v = self.cnt[inc]
        self.q[eng].append(lambda E, fn=fn, h=self.sems[inc]: fn(E).then_inc(h, 1))
        self.waited[eng][inc] = v
        return (inc, v)

    def dma(self, eng, qname, out, in_, waits=()):
        lanes = self.qlanes[qname]
        lane = lanes[self.qrr[qname] % len(lanes)]
        self.qrr[qname] += 1
        for ev in waits:
            self.wait(eng, ev)
        self.wait(eng, (lane, self.cnt[lane]))
        self.cnt[lane] += 16
        v = self.cnt[lane]
        self.q[eng].append(
            lambda E, out=out, in_=in_, h=self.sems[lane]: E.dma_start(out, in_).then_inc(h, 16))
        return (lane, v)

    def drain(self, eng):
        self.q[eng].append(lambda E: E.drain())


class Pool:
    def __init__(self, tensors):
        self.slots = [{"t": t, "ev": None} for t in tensors]
        self.i = 0

    def acquire(self):
        s = self.slots[self.i % len(self.slots)]
        self.i += 1
        return s


def mm_group(kb, psum_ap, pairs, waits_first=()):
    n = len(pairs)
    ev = None
    for idx, (lt, rs, ws) in enumerate(pairs):
        waits = list(ws) + (list(waits_first) if idx == 0 else [])
        ev = kb.op(
            "tensor",
            lambda E, o=psum_ap, l=lt, r=rs, st=(idx == 0), sp=(idx == n - 1):
                E.matmul(o, l, r, start=st, stop=sp),
            waits=waits,
            inc="pe" if idx == n - 1 else None)
    return ev


def transp(kb, out_ap, in_ap, ident_ap, waits=()):
    return kb.op(
        "tensor",
        lambda E, o=out_ap, i=in_ap, d=ident_ap: E.transpose(o, i, d),
        waits=waits, inc="pe")


def vcopy(kb, dst, src, waits=()):
    return kb.op("vector", lambda E, d=dst, s=src: E.tensor_copy(d, s),
                 waits=waits, inc="v")


def build_nc(ntok=2048, dbg_skip=()):
    assert ntok % 128 == 0
    NT = ntok // 128
    W = ntok + TC0

    nc = bass.Bass("TRN2", target_bir_lowering=False)

    xfm_d = nc.dram_tensor("xfm", [D_MODEL, ntok + TC0], F32, kind="ExternalInput")
    w_inT = nc.dram_tensor("w_inT", [D_MODEL, 2 * D_INNER], F32, kind="ExternalInput")
    w_outT = nc.dram_tensor("w_outT", [D_INNER, D_MODEL], F32, kind="ExternalInput")
    w1_d = nc.dram_tensor("w1", [NE, D_MODEL, TWOI], F32, kind="ExternalInput")
    w2_d = nc.dram_tensor("w2", [NE, I_FFN, D_MODEL], F32, kind="ExternalInput")
    sw1T = nc.dram_tensor("sw1T", [D_MODEL, I_FFN], F32, kind="ExternalInput")
    sw2T = nc.dram_tensor("sw2T", [D_MODEL, I_FFN], F32, kind="ExternalInput")
    sw3T = nc.dram_tensor("sw3T", [I_FFN, D_MODEL], F32, kind="ExternalInput")
    gwT_d = nc.dram_tensor("gwT", [D_MODEL, NE], F32, kind="ExternalInput")
    cwk_d = nc.dram_tensor("cwk", [128, 128], F32, kind="ExternalInput")
    cb_d = nc.dram_tensor("cbcol", [128, 32], F32, kind="ExternalInput")
    dd_d = nc.dram_tensor("dcol", [128, 32], F32, kind="ExternalInput")
    idf_d = nc.dram_tensor("identf", [128, 128], F32, kind="ExternalInput")
    idb_d = nc.dram_tensor("identb", [128, 128], BF16, kind="ExternalInput")
    y_d = nc.dram_tensor("y", [D_MODEL, ntok], F32, kind="ExternalOutput")
    rows_d = nc.dram_tensor("rows_scratch", [7, ntok + TC0], F32)
    rows8_d = nc.dram_tensor("rows8_scratch", [NE, ntok], BF16)

    kb = KB(nc)
    with ExitStack() as st_:
        for s in ("pe", "v", "sc"):
            kb.add_sem(st_, s)
        for qn, nl in QUEUES.items():
            kb.add_queue(st_, qn, nl)

        A = [st_.enter_context(nc.sbuf_tensor(f"A{c}", [128, W], BF16)) for c in range(NCD)]
        B = [st_.enter_context(nc.sbuf_tensor(f"B{c}", [128, ntok], BF16)) for c in range(NCD)]
        identf = st_.enter_context(nc.sbuf_tensor("identf_s", [128, 128], F32))
        identb = st_.enter_context(nc.sbuf_tensor("identb_s", [128, 128], BF16))
        ones_c = st_.enter_context(nc.sbuf_tensor("ones_c", [128, 1], BF16))
        ones_r = st_.enter_context(nc.sbuf_tensor("ones_r", [1, 128], BF16))
        cwk = st_.enter_context(nc.sbuf_tensor("cwk_s", [128, 128], F32))
        cbc = st_.enter_context(nc.sbuf_tensor("cbc", [128, 32], F32))
        ddc = st_.enter_context(nc.sbuf_tensor("ddc", [128, 32], F32))
        epsc = st_.enter_context(nc.sbuf_tensor("epsc", [128, 1], F32))
        wxi = [st_.enter_context(nc.sbuf_tensor(f"wxi{i}", [128, NCD * 128], BF16)) for i in range(2)]
        wga = [st_.enter_context(nc.sbuf_tensor(f"wga{i}", [128, NCD * 128], BF16)) for i in range(2)]


        pmT = [st_.enter_context(nc.psum_tensor(f"pm{i}", [128, 512], F32)) for i in range(8)]
        pm = Pool(pmT)

        p01 = ExitStack()
        ba0 = p01.enter_context(nc.sbuf_tensor("ba0", [128, W], BF16))
        bb_0 = p01.enter_context(nc.sbuf_tensor("bb_0", [128, W], BF16))
        s1bt = p01.enter_context(nc.sbuf_tensor("s1bt", [128, W], BF16))
        with ExitStack() as p0:
            sqf = [p0.enter_context(nc.sbuf_tensor(f"sqf{i}", [128, W], BF16)) for i in range(2)]
            rA0 = p0.enter_context(nc.sbuf_tensor("rA0", [1, W], F32))
            rB0 = p0.enter_context(nc.sbuf_tensor("rB0", [1, W], F32))
            pfS = p0.enter_context(nc.sbuf_tensor("pfS", [128, 32], F32))
            pfQ = p0.enter_context(nc.sbuf_tensor("pfQ", [128, 32], F32))
            pfT = p0.enter_context(nc.sbuf_tensor("pfT", [128, 32], F32))
            qrem = p0.enter_context(nc.sbuf_tensor("qrem", [128, 64], BF16))

            kb.begin()
            ev_id1 = kb.dma("sync", "sx", identf[:, :], idf_d[:, :])
            ev_id2 = kb.dma("sync", "sx", identb[:, :], idb_d[:, :])
            ev_cwk = kb.dma("sync", "sx", cwk[:, :], cwk_d[:, :])
            ev_cb = kb.dma("sync", "sx", cbc[:, :], cb_d[:, :])
            ev_dd = kb.dma("sync", "sx", ddc[:, :], dd_d[:, :])
            ev_eps = kb.op("vector", lambda E: E.memset(epsc[:, :], EPS), inc="v")
            ev_ones = kb.op("vector", lambda E: E.memset(ones_c[:, :], 1.0), inc="v")
            ev_ones_r = kb.op("vector", lambda E: E.memset(ones_r[:, :], 1.0), inc="v")

            ld_ev = []
            for c in range(NCD):
                ld_ev.append(kb.dma("gpsimd", "sw", A[c][:, :],
                                    xfm_d[c * 128:(c + 1) * 128, :]))
            pf_xi = kb.dma("gpsimd", "sw",
                           wxi[0][:, :].rearrange("p (k j) -> p k j", j=128),
                           dap(w_inT, 0, [[2 * D_INNER, 128], [128 * 2 * D_INNER, NCD], [1, 128]]))
            pf_ga = kb.dma("gpsimd", "sw",
                           wga[0][:, :].rearrange("p (k j) -> p k j", j=128),
                           dap(w_inT, D_INNER, [[2 * D_INNER, 128], [128 * 2 * D_INNER, NCD], [1, 128]]))
            main_w = min(W - W % 512, W) if W > 512 else W
            mblks = tblks(main_w)
            slots_x = [pm.acquire() for _ in range(len(mblks))]
            slots_q = [pm.acquire() for _ in range(len(mblks))]
            sq_war = [None, None]
            sq_evs = [None] * NCD
            qr_evs = [None] * NCD
            for k in range(NCD):
                qb = sqf[k % 2]
                if k % 2 == 0:
                    ev_q = kb.op("scalar",
                                 lambda E, k=k, qb=qb: E.activation(qb[:, :], A[k][:, :],
                                                                    AF.Square),
                                 waits=[ld_ev[k], sq_war[k % 2]], inc="sc")
                else:
                    kb.drain("vector")
                    ev_q = kb.op("vector",
                                 lambda E, k=k, qb=qb: E.tensor_mul(qb[:, :], A[k][:, :],
                                                                    A[k][:, :]),
                                 waits=[ld_ev[k], sq_war[k % 2]], inc="v")
                sq_evs[k] = ev_q
                for bidx, (t0, tw) in enumerate(mblks):
                    kb.op("tensor",
                          lambda E, o=slots_x[bidx]["t"][0:1, 0:tw],
                                 r=A[k][:, t0:t0 + tw], st=(k == 0), sp=(k == NCD - 1):
                              E.matmul(o, ones_c[:, 0:1], r, start=st, stop=sp),
                          waits=[ld_ev[k], ev_ones] + ([slots_x[bidx]["ev"]] if k == 0 else []),
                          inc="pe" if k == NCD - 1 else None)
                    if k == NCD - 1:
                        ev_e = vcopy(kb, rA0[0:1, t0:t0 + tw], slots_x[bidx]["t"][0:1, 0:tw],
                                     waits=[("pe", kb.cnt["pe"])])
                        slots_x[bidx]["ev"] = ev_e
                    ev_pe = kb.op(
                        "tensor",
                        lambda E, o=slots_q[bidx]["t"][0:1, 0:tw],
                               r=qb[:, t0:t0 + tw], st=(k == 0), sp=(k == NCD - 1):
                            E.matmul(o, ones_c[:, 0:1], r, start=st, stop=sp),
                        waits=[ev_q] + ([slots_q[bidx]["ev"]] if k == 0 else []),
                        inc="pe")
                    if k == NCD - 1:
                        ev_e = vcopy(kb, rB0[0:1, t0:t0 + tw], slots_q[bidx]["t"][0:1, 0:tw],
                                     waits=[ev_pe])
                        slots_q[bidx]["ev"] = ev_e
                sq_war[k % 2] = ev_pe
                if main_w < W:
                    kb.drain("vector")
                    qr_evs[k] = kb.op("vector",
                                      lambda E, k=k, twr=W - main_w: E.tensor_mul(
                                          qrem[:, k * twr:(k + 1) * twr],
                                          A[k][:, main_w:W], A[k][:, main_w:W]),
                                      waits=[ld_ev[k]], inc="v")
            if main_w < W:
                t0r, twr = main_w, W - main_w
                s_x = pm.acquire()
                s_q = pm.acquire()
                for k in range(NCD):
                    kb.op("tensor",
                          lambda E, o=s_x["t"][0:1, 0:twr], r=A[k][:, t0r:t0r + twr],
                                 st=(k == 0), sp=(k == NCD - 1):
                              E.matmul(o, ones_c[:, 0:1], r, start=st, stop=sp),
                          waits=[s_x["ev"]] if k == 0 else [],
                          inc="pe" if k == NCD - 1 else None)
                    if k == NCD - 1:
                        ev_e = vcopy(kb, rA0[0:1, t0r:t0r + twr], s_x["t"][0:1, 0:twr],
                                     waits=[("pe", kb.cnt["pe"])])
                        s_x["ev"] = ev_e
                    ev_pe = kb.op(
                        "tensor",
                        lambda E, o=s_q["t"][0:1, 0:twr],
                               st=(k == 0), sp=(k == NCD - 1), k=k:
                            E.matmul(o, ones_c[:, 0:1], qrem[:, k * twr:(k + 1) * twr],
                                     start=st, stop=sp),
                        waits=[qr_evs[k]] + ([s_q["ev"]] if k == 0 else []),
                        inc="pe")
                    if k == NCD - 1:
                        ev_e = vcopy(kb, rB0[0:1, t0r:t0r + twr], s_q["t"][0:1, 0:twr],
                                     waits=[ev_pe])
                        s_q["ev"] = ev_e
            FF = next(f for f in range(max(1, W // 128), W + 1) if W % f == 0 and W // f <= 128)
            PP = W // FF
            ev_stS = kb.dma("gpsimd", "sw", dap(rows_d, 5 * W, [[1, W]]), rA0[0:1, 0:W],
                            waits=[("v", kb.cnt["v"])])
            ev_stQ = kb.dma("gpsimd", "sw", dap(rows_d, 6 * W, [[1, W]]), rB0[0:1, 0:W],
                            waits=[("v", kb.cnt["v"])])
            ev_ldS = kb.dma("gpsimd", "sw", pfS[0:PP, 0:FF],
                            dap(rows_d, 5 * W, [[FF, PP], [1, FF]]), waits=[ev_stS])
            ev_ldQ = kb.dma("gpsimd", "sw", pfQ[0:PP, 0:FF],
                            dap(rows_d, 6 * W, [[FF, PP], [1, FF]]), waits=[ev_stQ])
            kb.op("vector", lambda E: E.tensor_scalar(pfS[0:PP, 0:FF], pfS[0:PP, 0:FF],
                                                      1.0 / D_MODEL, None, ALU.mult),
                  waits=[ev_ldS])
            kb.op("vector", lambda E: E.tensor_scalar(pfQ[0:PP, 0:FF], pfQ[0:PP, 0:FF],
                                                      1.0 / D_MODEL, None, ALU.mult),
                  waits=[ev_ldQ])
            kb.drain("vector")
            ev_m1 = kb.op("vector", lambda E: E.tensor_mul(pfT[0:PP, 0:FF], pfS[0:PP, 0:FF],
                                                           pfS[0:PP, 0:FF]), inc="v")
            ev_stm = kb.dma("gpsimd", "sw", dap(rows_d, 0, [[FF, PP], [1, FF]]),
                            pfS[0:PP, 0:FF], waits=[ev_m1])
            kb.drain("vector")
            ev_var = kb.op("vector",
                           lambda E: E.tensor_sub(pfQ[0:PP, 0:FF], pfQ[0:PP, 0:FF], pfT[0:PP, 0:FF]),
                           inc="v")
            ev_s1 = kb.op("scalar", lambda E: E.activation(pfQ[0:PP, 0:FF], pfQ[0:PP, 0:FF], AF.Sqrt,
                                                           bias=epsc[0:PP, 0:1], scale=1.0),
                          waits=[ev_var, ev_eps], inc="sc")
            ev_sts = kb.dma("gpsimd", "sw", dap(rows_d, W, [[FF, PP], [1, FF]]),
                            pfQ[0:PP, 0:FF], waits=[ev_s1])
            ev_r1 = kb.op("vector", lambda E: E.reciprocal(pfT[0:PP, 0:FF], pfQ[0:PP, 0:FF]),
                          waits=[ev_s1], inc="v")
            ev_str = kb.dma("gpsimd", "sw", dap(rows_d, 4 * W, [[FF, PP], [1, FF]]),
                            pfT[0:PP, 0:FF], waits=[ev_r1])
            ev_ba = kb.dma("gpsimd", "sw", ba0[:, :], dap(rows_d, 0, [[0, 128], [1, W]]),
                           waits=[ev_stm])
            ev_bb = kb.dma("gpsimd", "sw", bb_0[:, :], dap(rows_d, 4 * W, [[0, 128], [1, W]]),
                           waits=[ev_str])
            kb.end("p0")

        with ExitStack() as p1:
            wo = [p1.enter_context(nc.sbuf_tensor(f"wo{i}", [128, GK * 128], BF16)) for i in range(2)]
            m1r = p1.enter_context(nc.sbuf_tensor("m1r", [1, ntok], BF16))
            xi_sb = p1.enter_context(nc.sbuf_tensor("xi_sb", [128, W], BF16))
            cacc = p1.enter_context(nc.sbuf_tensor("cacc", [128, ntok], BF16))
            sg_m = p1.enter_context(nc.sbuf_tensor("sg_m", [128, ntok], BF16))
            H2 = [ba0[:, 0:ntok], bb_0[:, 0:ntok], s1bt[:, 0:ntok]] + [
                p1.enter_context(nc.sbuf_tensor(f"H2_{i}", [128, ntok], BF16))
                for i in range(GK - 3)]

            kb.begin()
            kb.wait("vector", ev_cwk)
            kb.wait("scalar", ev_cb)
            kb.wait("vector", ev_dd)
            app_ev = [None] * NCD
            for c in range(NCD):
                kb.drain("vector")
                kb.op("vector",
                      lambda E, c=c: E.tensor_sub(A[c][:, :], A[c][:, :], ba0[:, :]))
                kb.drain("vector")
                app_ev[c] = kb.op("vector",
                                  lambda E, c=c: E.tensor_mul(A[c][:, :], A[c][:, :],
                                                              bb_0[:, :]),
                                  inc="v")
            ev_s1bt = kb.dma("gpsimd", "sw", s1bt[:, 0:ntok],
                             dap(rows_d, W + TC0, [[0, 128], [1, ntok]]), waits=[ev_sts])
            ev_m1r = kb.dma("gpsimd", "sw", m1r[0:1, 0:ntok],
                            dap(rows_d, TC0, [[1, ntok]]), waits=[ev_stm])
            binit_ev = [None] * NCD
            for c in range(NCD):
                kb.drain("vector")
                binit_ev[c] = kb.op("vector",
                                    lambda E, c=c: E.tensor_mul(B[c][:, :],
                                                                A[c][:, TC0:TC0 + ntok],
                                                                s1bt[:, 0:ntok]),
                                    waits=[app_ev[c], ev_s1bt], inc="v")
            wxi_cons = [None, None]
            wga_cons = [None, None]
            wo_cons = [None, None]
            b_fin = [None] * NCD
            h2_ev = [None] * GK
            h2_gate = None
            wo_pf = {}
            ev_sxc = None
            ev_h2_prev = None
            for g in range(NG):
                for dl in range(GK):
                    if dl == GK - 3:
                        for co_p in (0, 1):
                            wo_pf[co_p] = kb.dma(
                                "gpsimd", "sw",
                                wo[co_p % 2][:, :].rearrange("p (k j) -> p k j", j=128),
                                dap(w_outT, (g * GK * 128) * D_MODEL + co_p * 128,
                                    [[D_MODEL, 128], [128 * D_MODEL, GK], [1, 128]]),
                                waits=[wo_cons[co_p % 2]])
                    dc = g * GK + dl
                    bi = dc % 2
                    if dc == 0:
                        ld_xi, ld_ga = pf_xi, pf_ga
                    else:
                        ld_xi = kb.dma(
                            "gpsimd", "sw",
                            wxi[bi][:, :].rearrange("p (k j) -> p k j", j=128),
                            dap(w_inT, dc * 128,
                                [[2 * D_INNER, 128], [128 * 2 * D_INNER, NCD], [1, 128]]),
                            waits=[wxi_cons[bi]])
                        ld_ga = kb.dma(
                            "gpsimd", "sw",
                            wga[bi][:, :].rearrange("p (k j) -> p k j", j=128),
                            dap(w_inT, D_INNER + dc * 128,
                                [[2 * D_INNER, 128], [128 * 2 * D_INNER, NCD], [1, 128]]),
                            waits=[wga_cons[bi]])
                    xi_evs = []
                    last_pe = None
                    for (t0, tw) in tblks(W):
                        slot = pm.acquire()
                        pairs = [(wxi[bi][:, k * 128:(k + 1) * 128], A[k][:, t0:t0 + tw],
                                  [ld_xi, app_ev[k]]) for k in range(NCD)]
                        ev_pe = mm_group(kb, slot["t"][:, 0:tw], pairs, waits_first=[slot["ev"]])
                        ev_e = kb.op("scalar",
                                     lambda E, d=xi_sb[:, t0:t0 + tw], s=slot["t"][:, 0:tw]:
                                         E.copy(d, s),
                                     waits=[ev_pe], inc="sc")
                        slot["ev"] = ev_e
                        xi_evs.append(ev_e)
                        last_pe = ev_pe
                    wxi_cons[bi] = last_pe
                    sg_evs = []
                    for (t0, tw) in tblks(ntok):
                        slot = pm.acquire()
                        pairs = [(wga[bi][:, k * 128:(k + 1) * 128],
                                  A[k][:, TC0 + t0:TC0 + t0 + tw], [ld_ga, app_ev[k]])
                                 for k in range(NCD)]
                        ev_pe = mm_group(kb, slot["t"][:, 0:tw], pairs, waits_first=[slot["ev"]])
                        ev_e = kb.op("scalar",
                                     lambda E, d=sg_m[:, t0:t0 + tw], s=slot["t"][:, 0:tw]:
                                         E.activation(d, s, AF.Silu),
                                     waits=[ev_pe, ev_h2_prev], inc="sc")
                        slot["ev"] = ev_e
                        sg_evs.append(ev_e)
                        last_pe = ev_pe
                    wga_cons[bi] = last_pe
                    kb.drain("vector")
                    kb.op("vector",
                          lambda E, dc=dc: E.tensor_single_scalar(
                              cacc[:, 0:ntok], xi_sb[:, 1:1 + ntok],
                              cwk[:, dc * 4:dc * 4 + 1], ALU.mult),
                          waits=[ev_sxc] + xi_evs)
                    ev_conv = None
                    for kk in range(1, 4):
                        kb.drain("vector")
                        ev_conv = kb.op("vector",
                                        lambda E, dc=dc, kk=kk: E.scalar_tensor_tensor(
                                            cacc[:, 0:ntok], xi_sb[:, 1 + kk:1 + kk + ntok],
                                            cwk[:, dc * 4 + kk:dc * 4 + kk + 1], cacc[:, 0:ntok],
                                            ALU.mult, ALU.add),
                                        inc="v" if kk == 3 else None)
                    ev_sxc = kb.op("scalar",
                                   lambda E, dc=dc: E.activation(cacc[:, 0:ntok], cacc[:, 0:ntok],
                                                                 AF.Silu,
                                                                 bias=cbc[:, dc:dc + 1], scale=1.0),
                                   waits=[ev_conv, ev_h2_prev], inc="sc")
                    kb.drain("vector")
                    ev_h2 = kb.op("vector",
                                  lambda E, dl=dl, dc=dc: E.scalar_tensor_tensor(
                                      H2[dl][:, 0:ntok], cacc[:, 0:ntok], ddc[:, dc:dc + 1],
                                      sg_m[:, 0:ntok], ALU.mult, ALU.mult),
                                  waits=[ev_sxc, h2_gate] + sg_evs, inc="v")
                    h2_ev[dl] = ev_h2
                    ev_h2_prev = ev_h2
                kb.drain("vector")
                last_pe = None
                for co in range(NCD):
                    bi = co % 2
                    if co in wo_pf:
                        ld_wo = wo_pf.pop(co)
                    else:
                        ld_wo = kb.dma(
                            "gpsimd", "sw",
                            wo[bi][:, :].rearrange("p (k j) -> p k j", j=128),
                            dap(w_outT, (g * GK * 128) * D_MODEL + co * 128,
                                [[D_MODEL, 128], [128 * D_MODEL, GK], [1, 128]]),
                            waits=[wo_cons[bi]])
                    for (t0, tw) in tblks(ntok):
                        slot = pm.acquire()
                        pairs = [(wo[bi][:, kk * 128:(kk + 1) * 128], H2[kk][:, t0:t0 + tw],
                                  [ld_wo, h2_ev[kk]]) for kk in range(GK)]
                        if g == 0:
                            pairs.append((ones_r[0:1, 0:128], m1r[0:1, t0:t0 + tw],
                                          [ev_m1r, ev_ones_r]))
                        ev_pe = mm_group(kb, slot["t"][:, 0:tw], pairs, waits_first=[slot["ev"]])
                        ev_e = kb.op("vector",
                                     lambda E, d=B[co][:, t0:t0 + tw], s=slot["t"][:, 0:tw]:
                                         E.tensor_add(d, d, s),
                                     waits=[ev_pe, binit_ev[co]], inc="v")
                        slot["ev"] = ev_e
                        last_pe = ev_pe
                        if g == NG - 1:
                            b_fin[co] = ev_e
                    wo_cons[bi] = last_pe
                h2_gate = last_pe
            kb.end("p1")
        p01.close()

        with ExitStack() as p2:
            bb0 = p2.enter_context(nc.sbuf_tensor("bb0", [128, ntok], BF16))
            bb1 = p2.enter_context(nc.sbuf_tensor("bb1", [128, ntok], BF16))
            trt = p2.enter_context(nc.sbuf_tensor("trt", [128, ntok], BF16))
            sqb = [p2.enter_context(nc.sbuf_tensor(f"sqb{i}", [128, ntok], BF16)) for i in range(2)]
            rowA = p2.enter_context(nc.sbuf_tensor("rowA", [1, ntok], F32))
            rowB = p2.enter_context(nc.sbuf_tensor("rowB", [1, ntok], F32))
            pf2S = p2.enter_context(nc.sbuf_tensor("pf2S", [128, 16], F32))
            pf2Q = p2.enter_context(nc.sbuf_tensor("pf2Q", [128, 16], F32))
            pf2T = p2.enter_context(nc.sbuf_tensor("pf2T", [128, 16], F32))
            gwsb = p2.enter_context(nc.sbuf_tensor("gwsb", [128, NCD * NE], BF16))
            cwf8 = p2.enter_context(nc.sbuf_tensor("cwf8s", [NE, ntok], BF16))
            lg = p2.enter_context(nc.sbuf_tensor("lg", [NE, ntok], F32))
            ltw = p2.enter_context(nc.sbuf_tensor("ltw", [128, NT * NE], F32))
            msw = p2.enter_context(nc.sbuf_tensor("msw", [128, NT * NE], F32))
            ctw = p2.enter_context(nc.sbuf_tensor("ctw", [128, NT * NE], F32))
            gs = p2.enter_context(nc.sbuf_tensor("gs", [128, 3 * NT], F32))

            kb.begin()
            ev_gw = kb.dma("gpsimd", "sw",
                           gwsb[:, :].rearrange("p (k e) -> p k e", e=NE),
                           dap(gwT_d, 0, [[NE, 128], [128 * NE, NCD], [1, NE]]))
            pf_s1 = kb.dma("gpsimd", "sw",
                           wxi[0][:, :].rearrange("p (k j) -> p k j", j=128),
                           dap(sw1T, 0, [[I_FFN, 128], [128 * I_FFN, NCD], [1, 128]]))
            pf_s2 = kb.dma("gpsimd", "sw",
                           wga[0][:, :].rearrange("p (k j) -> p k j", j=128),
                           dap(sw2T, 0, [[I_FFN, 128], [128 * I_FFN, NCD], [1, 128]]))

            def bcast(row_idx, dst, waits=()):
                return [kb.dma("gpsimd", "sw", dst[:, :],
                               dap(rows_d, row_idx * W + TC0, [[0, 128], [1, ntok]]),
                               waits=list(waits))]

            recon_ev = b_fin
            kb.drain("vector")
            blks = tblks(ntok)
            nblk = len(blks)
            slots_x = [pm.acquire() for _ in range(nblk)]
            slots_q = [pm.acquire() for _ in range(nblk)]
            war_sq = [None, None]
            for k in range(NCD):
                qb = sqb[k % 2]
                if k % 2 == 0:
                    ev_q = kb.op("scalar",
                                 lambda E, k=k, qb=qb: E.activation(qb[:, :], B[k][:, :],
                                                                    AF.Square),
                                 waits=[recon_ev[k], war_sq[k % 2]], inc="sc")
                else:
                    kb.drain("vector")
                    ev_q = kb.op("vector",
                                 lambda E, k=k, qb=qb: E.tensor_mul(qb[:, :], B[k][:, :],
                                                                    B[k][:, :]),
                                 waits=[recon_ev[k], war_sq[k % 2]], inc="v")
                for bidx, (t0, tw) in enumerate(blks):
                    kb.op("tensor",
                          lambda E, o=slots_x[bidx]["t"][0:1, 0:tw],
                                 r=B[k][:, t0:t0 + tw], st=(k == 0), sp=(k == NCD - 1):
                              E.matmul(o, ones_c[:, 0:1], r, start=st, stop=sp),
                          waits=[recon_ev[k]] + ([slots_x[bidx]["ev"]] if k == 0 else []),
                          inc="pe" if k == NCD - 1 else None)
                    if k == NCD - 1:
                        ev_e = vcopy(kb, rowA[0:1, t0:t0 + tw], slots_x[bidx]["t"][0:1, 0:tw],
                                     waits=[("pe", kb.cnt["pe"])])
                        slots_x[bidx]["ev"] = ev_e
                    ev_pe = kb.op(
                        "tensor",
                        lambda E, o=slots_q[bidx]["t"][0:1, 0:tw],
                               r=qb[:, t0:t0 + tw], st=(k == 0), sp=(k == NCD - 1):
                            E.matmul(o, ones_c[:, 0:1], r, start=st, stop=sp),
                        waits=[ev_q] + ([slots_q[bidx]["ev"]] if k == 0 else []),
                        inc="pe")
                    if k == NCD - 1:
                        ev_e = vcopy(kb, rowB[0:1, t0:t0 + tw], slots_q[bidx]["t"][0:1, 0:tw],
                                     waits=[ev_pe])
                        slots_q[bidx]["ev"] = ev_e
                war_sq[k % 2] = ev_pe
            ev_stS2 = kb.dma("gpsimd", "sw", dap(rows_d, 5 * W, [[1, ntok]]), rowA[0:1, :],
                             waits=[("v", kb.cnt["v"])])
            ev_stQ2 = kb.dma("gpsimd", "sw", dap(rows_d, 6 * W, [[1, ntok]]), rowB[0:1, :],
                             waits=[("v", kb.cnt["v"])])
            ev_ldS2 = kb.dma("gpsimd", "sw", pf2S[0:128, 0:NT],
                             dap(rows_d, 5 * W, [[NT, 128], [1, NT]]), waits=[ev_stS2])
            ev_ldQ2 = kb.dma("gpsimd", "sw", pf2Q[0:128, 0:NT],
                             dap(rows_d, 6 * W, [[NT, 128], [1, NT]]), waits=[ev_stQ2])
            kb.op("vector", lambda E: E.tensor_scalar(pf2S[:, 0:NT], pf2S[:, 0:NT],
                                                      1.0 / D_MODEL, None, ALU.mult),
                  waits=[ev_ldS2])
            kb.op("vector", lambda E: E.tensor_scalar(pf2Q[:, 0:NT], pf2Q[:, 0:NT],
                                                      1.0 / D_MODEL, None, ALU.mult),
                  waits=[ev_ldQ2])
            kb.drain("vector")
            ev_m2 = kb.op("vector", lambda E: E.tensor_mul(pf2T[:, 0:NT], pf2S[:, 0:NT],
                                                           pf2S[:, 0:NT]), inc="v")
            ev_st_m2 = kb.dma("gpsimd", "sw", dap(rows_d, 2 * W + TC0, [[NT, 128], [1, NT]]),
                              pf2S[0:128, 0:NT], waits=[ev_m2])
            kb.drain("vector")
            ev_var = kb.op("vector",
                           lambda E: E.tensor_sub(pf2Q[:, 0:NT], pf2Q[:, 0:NT], pf2T[:, 0:NT]),
                           inc="v")
            ev_sr = kb.op("scalar", lambda E: E.activation(pf2Q[:, 0:NT], pf2Q[:, 0:NT], AF.Sqrt,
                                                           bias=epsc[0:128, 0:1], scale=1.0),
                          waits=[ev_var], inc="sc")
            ev_r2 = kb.op("vector", lambda E: E.reciprocal(pf2T[:, 0:NT], pf2Q[:, 0:NT]),
                          waits=[ev_sr], inc="v")
            ev_st_r2 = kb.dma("gpsimd", "sw", dap(rows_d, 3 * W + TC0, [[NT, 128], [1, NT]]),
                              pf2T[0:128, 0:NT], waits=[ev_r2])
            ev_m2b = bcast(2, bb1, waits=[ev_st_m2])
            ev_r2b = bcast(3, bb0, waits=[ev_st_r2])
            xn2_ev = [None] * NCD
            for c in range(NCD):
                kb.drain("vector")
                kb.op("vector", lambda E, c=c: E.tensor_sub(trt[:, :], B[c][:, :], bb1[:, :]),
                      waits=ev_m2b)
                kb.drain("vector")
                xn2_ev[c] = kb.op("vector",
                                  lambda E, c=c: E.tensor_mul(A[c][:, TC0:TC0 + ntok],
                                                              trt[:, :], bb0[:, :]),
                                  waits=ev_r2b, inc="v")
            lg_evs = []
            for (t0, tw) in tblks(ntok):
                slot = pm.acquire()
                pairs = [(gwsb[:, k * NE:(k + 1) * NE], A[k][:, TC0 + t0:TC0 + t0 + tw],
                          [ev_gw, xn2_ev[k]]) for k in range(NCD)]
                ev_pe = mm_group(kb, slot["t"][0:NE, 0:tw], pairs, waits_first=[slot["ev"]])
                ev_e = vcopy(kb, lg[0:NE, t0:t0 + tw], slot["t"][0:NE, 0:tw], waits=[ev_pe])
                slot["ev"] = ev_e
                lg_evs.append(ev_e)
            ev_ltw = None
            for i in range(NT):
                s_t = pm.acquire()
                ev_t = transp(kb, s_t["t"][:, 0:NE],
                              lg[0:NE, i * 128:(i + 1) * 128], identf[0:NE, 0:NE],
                              waits=[s_t["ev"], lg_evs[(i * 128) // 512]])
                ev_ltw = vcopy(kb, ltw[:, i * NE:(i + 1) * NE], s_t["t"][:, 0:NE], waits=[ev_t])
                s_t["ev"] = ev_ltw

            def r3(t):
                return t[:, 0:NT * NE].rearrange("p (i e) -> p i e", e=NE)

            def b3(gslice):
                return gslice.unsqueeze(2).broadcast_to([128, NT, NE])

            def vop(fn, waits=(), inc=None, dr=True):
                if dr:
                    kb.drain("vector")
                return kb.op("vector", fn, waits=waits, inc=inc)

            g0, g1, g2 = gs[:, 0:NT], gs[:, NT:2 * NT], gs[:, 2 * NT:3 * NT]
            vop(lambda E: E.reduce_max(g0, r3(ltw), axis=AX.X), waits=[ev_ltw])
            ev_sh = vop(lambda E: E.tensor_sub(r3(ltw), r3(ltw), b3(g0)), inc="v")
            ev_ex = kb.op("scalar", lambda E: E.activation(ltw[:, 0:NT * NE], ltw[:, 0:NT * NE],
                                                           AF.Exp),
                          waits=[ev_sh], inc="sc")
            vop(lambda E: E.reduce_sum(g1, r3(ltw), axis=AX.X), waits=[ev_ex], dr=False)
            vop(lambda E: E.reciprocal(g2, g1))
            ev_p = vop(lambda E: E.tensor_mul(r3(ltw), r3(ltw), b3(g2)), inc="v")
            vop(lambda E: E.reduce_max(g0, r3(ltw), axis=AX.X))
            vop(lambda E: E.tensor_tensor(r3(msw), r3(ltw), b3(g0), ALU.is_equal))
            vop(lambda E: E.scalar_tensor_tensor(ctw[:, 0:NT * NE], msw[:, 0:NT * NE], -1e30,
                                                 ltw[:, 0:NT * NE], ALU.mult, ALU.add))
            vop(lambda E: E.reduce_max(g1, r3(ctw), axis=AX.X))
            vop(lambda E: E.tensor_tensor(r3(ctw), r3(ctw), b3(g1), ALU.is_equal))
            vop(lambda E: E.tensor_add(g2, g0, g1))
            vop(lambda E: E.tensor_scalar(g2, g2, 1e-20, None, ALU.add))
            vop(lambda E: E.reciprocal(g2, g2))
            vop(lambda E: E.tensor_mul(g0, g0, g2))
            kb.op("vector", lambda E: E.tensor_mul(g1, g1, g2))
            vop(lambda E: E.tensor_mul(r3(msw), r3(msw), b3(g0)))
            kb.op("vector", lambda E: E.tensor_mul(r3(ctw), r3(ctw), b3(g1)))
            ev_cw = vop(lambda E: E.tensor_add(msw[:, 0:NT * NE], msw[:, 0:NT * NE],
                                               ctw[:, 0:NT * NE]), inc="v")
            for i in range(NT):
                s_b = pm.acquire()
                ev_t2 = transp(kb, s_b["t"][0:NE, 0:128], msw[:, i * NE:(i + 1) * NE],
                               identf[0:128, 0:128], waits=[s_b["ev"], ev_cw])
                ev_e2 = vcopy(kb, cwf8[0:NE, i * 128:(i + 1) * 128], s_b["t"][0:NE, 0:128],
                              waits=[ev_t2])
                s_b["ev"] = ev_e2
            ev_cwf = kb.dma("gpsimd", "sw", rows8_d[0:NE, 0:ntok], cwf8[0:NE, 0:ntok],
                            waits=[("v", kb.cnt["v"])])
            kb.end("p2")

        with ExitStack() as p3:
            P = [p3.enter_context(nc.sbuf_tensor(f"P{i}", [128, ntok], BF16)) for i in range(NCI)]
            sgE = [p3.enter_context(nc.sbuf_tensor(f"sgE{i}", [128, ntok], BF16)) for i in range(2)]
            cwb = p3.enter_context(nc.sbuf_tensor("cwb", [128, ntok], BF16))
            w2b = [p3.enter_context(nc.sbuf_tensor(f"w2b_{i}", [128, NCI * 128], BF16)) for i in range(2)]

            kb.begin()
            wg_cons = [None, None]
            wu_cons = [None, None]
            w2_cons = [None, None]
            sg_war = [None, None]
            st_war = [None, None]
            ev_he_last = None
            ev_sg = None
            ev_pmul_prev = None
            for ic in range(NCI if 'shared' not in dbg_skip else 0):
                bi = ic % 2
                if ic == 0:
                    ld1, ld2 = pf_s1, pf_s2
                else:
                    ld1 = kb.dma("gpsimd", "sw",
                                 wxi[bi][:, :].rearrange("p (k j) -> p k j", j=128),
                                 dap(sw1T, ic * 128, [[I_FFN, 128], [128 * I_FFN, NCD], [1, 128]]),
                                 waits=[wg_cons[bi]])
                    ld2 = kb.dma("gpsimd", "sw",
                                 wga[bi][:, :].rearrange("p (k j) -> p k j", j=128),
                                 dap(sw2T, ic * 128, [[I_FFN, 128], [128 * I_FFN, NCD], [1, 128]]),
                                 waits=[wu_cons[bi]])
                last_pe = None
                for (t0, tw) in tblks(ntok):
                    slot = pm.acquire()
                    pairs = [(wxi[bi][:, k * 128:(k + 1) * 128],
                              A[k][:, TC0 + t0:TC0 + t0 + tw], [ld1]) for k in range(NCD)]
                    ev_pe = mm_group(kb, slot["t"][:, 0:tw], pairs, waits_first=[slot["ev"]])
                    sgb = sgE[ic % 2]
                    ev_e = kb.op("scalar",
                                 lambda E, d=sgb[:, t0:t0 + tw], s=slot["t"][:, 0:tw]:
                                     E.activation(d, s, AF.Silu),
                                 waits=[ev_pe, sg_war[ic % 2]], inc="sc")
                    slot["ev"] = ev_e
                    ev_sg = ev_e
                    last_pe = ev_pe
                wg_cons[bi] = last_pe
                for (t0, tw) in tblks(ntok):
                    slot = pm.acquire()
                    pairs = [(wga[bi][:, k * 128:(k + 1) * 128],
                              A[k][:, TC0 + t0:TC0 + t0 + tw], [ld2]) for k in range(NCD)]
                    ev_pe = mm_group(kb, slot["t"][:, 0:tw], pairs, waits_first=[slot["ev"]])
                    ev_e = vcopy(kb, P[ic][:, t0:t0 + tw], slot["t"][:, 0:tw], waits=[ev_pe])
                    slot["ev"] = ev_e
                    last_pe = ev_pe
                wu_cons[bi] = last_pe
                kb.drain("vector")
                ev_pmul_prev = kb.op("vector",
                                     lambda E, ic=ic, sgb=sgE[ic % 2]: E.tensor_mul(
                                         P[ic][:, :], P[ic][:, :], sgb[:, :]),
                                     waits=[ev_sg], inc="v")
                sg_war[ic % 2] = ev_pmul_prev
            p_gate = None
            kb.drain("vector")
            for co in range(NCD):
                bi = co % 2
                ld3 = kb.dma("gpsimd", "sw",
                             w2b[bi][:, :].rearrange("p (k j) -> p k j", j=128),
                             dap(sw3T, co * 128, [[D_MODEL, 128], [128 * D_MODEL, NCI], [1, 128]]),
                             waits=[w2_cons[bi]])
                for (t0, tw) in tblks(ntok):
                    slot = pm.acquire()
                    pairs = [(w2b[bi][:, k * 128:(k + 1) * 128], P[k][:, t0:t0 + tw],
                              [ld3, ev_pmul_prev]) for k in range(NCI)]
                    ev_pe = mm_group(kb, slot["t"][:, 0:tw], pairs, waits_first=[slot["ev"]])
                    ev_e = kb.op("vector",
                                 lambda E, d=B[co][:, t0:t0 + tw], s=slot["t"][:, 0:tw]:
                                     E.tensor_add(d, d, s),
                                 waits=[ev_pe], inc="v")
                    slot["ev"] = ev_e
                    p_gate = ev_pe
                w2_cons[bi] = p_gate
            ev_sg = None
            ne_run = NE
            for d in dbg_skip:
                if d.startswith('ne='):
                    ne_run = int(d[3:])
            if 'experts' in dbg_skip:
                ne_run = 0
            for e in range(ne_run):
                kb.drain("vector")
                cwb_evs = [kb.dma("gpsimd", "sw", cwb[:, :],
                                  dap(rows8_d, e * ntok, [[0, 128], [1, ntok]]),
                                  waits=[ev_cwf, ev_pmul_prev])]
                p_evs = [None] * NCI
                for J in range(NCI):
                    bi = J % 2
                    ldg = kb.dma("gpsimd", "sw",
                                 wxi[bi][:, :].rearrange("p (k j) -> p k j", j=128),
                                 dap(w1_d, e * D_MODEL * TWOI + J * 128,
                                     [[TWOI, 128], [128 * TWOI, NCD], [1, 128]]),
                                 waits=[wg_cons[bi]])
                    ldu = kb.dma("gpsimd", "sw",
                                 wga[bi][:, :].rearrange("p (k j) -> p k j", j=128),
                                 dap(w1_d, e * D_MODEL * TWOI + I_FFN + J * 128,
                                     [[TWOI, 128], [128 * TWOI, NCD], [1, 128]]),
                                 waits=[wu_cons[bi]])
                    last_pe = None
                    for (t0, tw) in tblks(ntok):
                        slot = pm.acquire()
                        pairs = [(wxi[bi][:, k * 128:(k + 1) * 128],
                                  A[k][:, TC0 + t0:TC0 + t0 + tw], [ldg]) for k in range(NCD)]
                        ev_pe = mm_group(kb, slot["t"][:, 0:tw], pairs, waits_first=[slot["ev"]])
                        sgb = sgE[J % 2]
                        ev_e = kb.op("scalar",
                                     lambda E, d=sgb[:, t0:t0 + tw], s=slot["t"][:, 0:tw]:
                                         E.activation(d, s, AF.Silu),
                                     waits=[ev_pe, sg_war[J % 2]], inc="sc")
                        slot["ev"] = ev_e
                        ev_sg = ev_e
                        last_pe = ev_pe
                    wg_cons[bi] = last_pe
                    for (t0, tw) in tblks(ntok):
                        slot = pm.acquire()
                        pairs = [(wga[bi][:, k * 128:(k + 1) * 128],
                                  A[k][:, TC0 + t0:TC0 + t0 + tw], [ldu]) for k in range(NCD)]
                        ev_pe = mm_group(kb, slot["t"][:, 0:tw], pairs, waits_first=[slot["ev"]])
                        ev_e = vcopy(kb, P[J][:, t0:t0 + tw], slot["t"][:, 0:tw],
                                     waits=[ev_pe, p_gate])
                        slot["ev"] = ev_e
                        last_pe = ev_pe
                    wu_cons[bi] = last_pe
                    ev_he_last = last_pe
                    kb.drain("vector")
                    kb.op("vector", lambda E, J=J, sgb=sgE[J % 2]: E.tensor_mul(
                              P[J][:, :], P[J][:, :], sgb[:, :]),
                          waits=[ev_sg], inc="v")
                    sg_war[J % 2] = (kb.cnt["v"] and ("v", kb.cnt["v"]))
                    kb.drain("vector")
                    ev_pm = kb.op("vector",
                                  lambda E, J=J: E.tensor_mul(P[J][:, :], P[J][:, :], cwb[:, :]),
                                  waits=cwb_evs, inc="v")
                    p_evs[J] = ev_pm
                    ev_pmul_prev = ev_pm
                kb.drain("vector")
                for cb_i in range(NCD):
                    bi = cb_i % 2
                    ld2 = kb.dma("gpsimd", "sw",
                                 w2b[bi][:, :].rearrange("p (k j) -> p k j", j=128),
                                 dap(w2_d, e * I_FFN * D_MODEL + cb_i * 128,
                                     [[D_MODEL, 128], [128 * D_MODEL, NCI], [1, 128]]),
                                 waits=[w2_cons[bi]])
                    for (t0, tw) in tblks(ntok):
                        slot = pm.acquire()
                        pairs = [(w2b[bi][:, k * 128:(k + 1) * 128], P[k][:, t0:t0 + tw],
                                  [ld2, p_evs[k]]) for k in range(NCI)]
                        ev_pe = mm_group(kb, slot["t"][:, 0:tw], pairs, waits_first=[slot["ev"]])
                        ev_e = kb.op("vector",
                                     lambda E, d=B[cb_i][:, t0:t0 + tw], s=slot["t"][:, 0:tw]:
                                         E.tensor_add(d, d, s),
                                     waits=[ev_pe], inc="v")
                        slot["ev"] = ev_e
                        p_gate = ev_pe
                    w2_cons[bi] = p_gate
                    if e == ne_run - 1:
                        view = A[cb_i][:, :].bitcast(F32)
                        fw = view.shape[1]
                        off = 0
                        while off < ntok:
                            w = min(fw, ntok - off)
                            kb.drain("vector")
                            ev_cast = kb.op(
                                "vector",
                                lambda E, d=view[:, 0:w], s=B[cb_i][:, off:off + w]:
                                    E.tensor_copy(d, s),
                                waits=[ev_e, ev_he_last, st_war[cb_i % 2]], inc="v")
                            st_war[cb_i % 2] = kb.dma(
                                "sync", "sst", y_d[cb_i * 128:(cb_i + 1) * 128, off:off + w],
                                view[:, 0:w], waits=[ev_cast])
                            off += w
            kb.end("p3")

        if ne_run != NE:
            kb.begin()
            for c in range(NCD):
                kb.dma("gpsimd", "sw", y_d[c * 128:(c + 1) * 128, :], B[c][:, :])
            kb.end("p4")

    return nc


def _stage(inputs):
    x = np.ascontiguousarray(np.asarray(inputs["L0_attn_x"], dtype=np.float32))
    w_in = np.asarray(inputs["L0_attn_w_in"], dtype=np.float32)
    w_out = np.asarray(inputs["L0_attn_w_out"], dtype=np.float32)
    conv_w = np.asarray(inputs["L0_attn_conv_w"], dtype=np.float32)
    conv_b = np.asarray(inputs["L0_attn_conv_b"], dtype=np.float32)
    dvec = np.asarray(inputs["L0_attn_D"], dtype=np.float32)
    gate_w = np.asarray(inputs["L0_ffn_gate_w"], dtype=np.float32)
    w1 = np.ascontiguousarray(np.asarray(inputs["L0_ffn_w1"], dtype=np.float32))
    w2 = np.ascontiguousarray(np.asarray(inputs["L0_ffn_w2"], dtype=np.float32))
    sw1 = np.asarray(inputs["L0_ffn_sw1"], dtype=np.float32)
    sw2 = np.asarray(inputs["L0_ffn_sw2"], dtype=np.float32)
    sw3 = np.asarray(inputs["L0_ffn_sw3"], dtype=np.float32)

    shared = {
        "w_inT": np.ascontiguousarray(w_in.T),
        "w_outT": np.ascontiguousarray(w_out.T),
        "w1": w1,
        "w2": w2,
        "sw1T": np.ascontiguousarray(sw1.T),
        "sw2T": np.ascontiguousarray(sw2.T),
        "sw3T": np.ascontiguousarray(sw3.T),
        "gwT": np.ascontiguousarray(gate_w.T),
        "cwk": np.ascontiguousarray(
            conv_w[:, 0, :].reshape(32, 128, 4).transpose(1, 0, 2).reshape(128, 128)),
        "cbcol": np.ascontiguousarray(conv_b.reshape(32, 128).T),
        "dcol": np.ascontiguousarray(dvec.reshape(32, 128).T),
        "identf": np.eye(128, dtype=np.float32),
        "identb": np.eye(128, dtype=ml_dtypes.bfloat16),
    }
    in_maps = []
    bsz, tfull, _ = x.shape
    half = tfull // 2
    for core in range(8):
        b, h = core // 2, core % 2
        t0 = h * half
        arr = np.zeros((half + 4, D_MODEL), np.float32)
        arr[4:] = x[b, t0:t0 + half]
        if t0 > 0:
            arr[1:4] = x[b, t0 - 3:t0]
        in_maps.append(dict(shared, xfm=np.ascontiguousarray(arr.T)))
    return in_maps


def kernel(**inputs):
    in_maps = _stage(inputs)
    ntok = in_maps[0]["xfm"].shape[1] - TC0
    nc = build_nc(ntok)
    res = run_bass_kernel_spmd(nc, in_maps, core_ids=list(range(8)))
    x = np.asarray(inputs["L0_attn_x"])
    bsz, tfull, dm = x.shape
    half = tfull // 2
    out = np.empty((bsz, tfull, dm), np.float32)
    for core, r in enumerate(res.results):
        b, h = core // 2, core % 2
        out[b, h * half:(h + 1) * half, :] = r["y"].T
    return out
